# version 22
# baseline (speedup 1.0000x reference)
import math
import numpy as np
import concourse.bass as bass
import concourse.mybir as mybir
from concourse.bass_utils import run_bass_kernel_spmd

F32 = mybir.dt.float32
BF16 = mybir.dt.bfloat16
AF = mybir.ActivationFunctionType
ALU = mybir.AluOpType
AX = mybir.AxisListType

T = 2048
C = 1024
H = 16
D = 64
NCH = 8
NP = 8
NEG = -1.0e30
LN_EPS = 1e-5
CAS_T = [128.0, 64.0, 32.0, 16.0, 8.0, 4.0, 2.0, 1.0]

N_SUBLAYERS = 4
DBG = False

ENGINES = ("sync", "tensor", "vector", "scalar", "gpsimd")


_trig_cache = {}


def _trig_tables():
    if "v" in _trig_cache:
        return _trig_cache["v"]
    import jax
    import jax.numpy as jnp
    with jax.default_device(jax.devices("cpu")[0]):
        inv_freq = 1.0 / 10000.0 ** (jnp.arange(D, dtype=jnp.float32) / D)
        t_idx = jnp.arange(T, dtype=jnp.float32)
        freqs = t_idx[:, None] * inv_freq[None, :]
        cosP = np.asarray(jnp.cos(freqs)).T.copy()
        sinP = np.asarray(jnp.sin(freqs)).T.copy()
        inv_b = 10000.0 ** (-(jnp.arange(0, C, 2, dtype=jnp.float32) / C))
        emb = jnp.repeat(inv_b, 2)
        sinr = np.asarray(jnp.sin(T * emb))
        cosr = np.asarray(jnp.cos(T * emb))
    _trig_cache["v"] = (cosP, sinP, cosr, sinr)
    return _trig_cache["v"]


def host_tables():
    tb = {}
    cosP, sinP, cosr, sinr = _trig_tables()
    tb["cosP"] = np.vstack([cosP] * 2).astype(np.float32)
    tb["sinP"] = np.vstack([sinP] * 2).astype(np.float32)

    msin = np.where(np.arange(C) < C // 2, -sinr, sinr)
    tb["cosrow"] = cosr.astype(np.float32).reshape(1, C)
    tb["msinrow"] = msin.astype(np.float32).reshape(1, C)

    m = np.zeros((128, 128), np.float32)
    for p in range(128):
        m[p, :p] = NEG
    tb["maskT"] = m
    tb["ident"] = np.eye(128, dtype=np.float32)
    s2 = np.zeros((2, 128), np.float32)
    s2[0, 0:64] = 1.0
    s2[1, 64:128] = 1.0
    tb["sel2"] = s2
    s2t = np.zeros((128, 2), np.float32)
    s2t[0:64, 0] = 1.0
    s2t[64:128, 1] = 1.0
    tb["sel2T"] = s2t
    selh = np.zeros((H, NP * 128), np.float32)
    for p8 in range(NP):
        selh[2 * p8, p8 * 128:p8 * 128 + 64] = 1.0
        selh[2 * p8 + 1, p8 * 128 + 64:p8 * 128 + 128] = 1.0
    tb["selh"] = selh
    cv = np.zeros((128, 8), np.float32)
    for i, v in enumerate([0.5, 1.0, 1e-5, 0.125, -0.5, -1.0, 0.0, 2.0]):
        cv[:, i] = v
    tb["constv"] = cv
    tb["onesc"] = np.ones((128, 1), np.float32)
    tb["onesr"] = np.ones((1, 128), np.float32)
    tb["onest"] = np.ones((1, 1), np.float32)
    return tb


DMA_RING = 16


class Sched:

    def __init__(self):
        self.steps = {e: [] for e in ENGINES}
        self.count = {e: 0 for e in ENGINES}
        self.dma_idx = {e: 0 for e in ENGINES}
        self.sem_names = list(ENGINES)
        for e in ("sync", "gpsimd", "scalar"):
            for i in range(DMA_RING):
                self.sem_names.append(f"{e}_d{i}")

    def step(self, eng, fn, deps=(), dma=False):
        dd = []
        for t in _flat(deps):
            if t is None:
                continue
            de, dv, ddma = t
            dd.append((de, dv))
        if not dma and eng != "tensor" and self.count[eng] > 0:
            dd.append((eng, self.count[eng]))
        if dma:
            i = self.dma_idx[eng]
            self.dma_idx[eng] += 1
            slot = f"{eng}_d{i % DMA_RING}"
            gen = i // DMA_RING
            ring_wait = (slot, 16 * gen)
            self.steps[eng].append(("dma", fn, dd, slot, ring_wait))
            return (slot, 16 * (gen + 1), True)
        self.steps[eng].append(("op", fn, dd, None, None))
        self.count[eng] += 1
        return (eng, self.count[eng], False)

    def tok(self, eng):
        return (eng, self.count[eng], False)

    def emit(self, blk, sems):
        final = dict(self.count)
        dma_final = {}
        for e in ENGINES:
            n = self.dma_idx.get(e, 0)
            for i in range(min(n, DMA_RING)):
                cnt = (n - i + DMA_RING - 1) // DMA_RING
                dma_final[f"{e}_d{i}"] = 16 * cnt
        barrier = getattr(self, "setup_barrier", [])

        def body_for(name):
            def body(eng):
                hw = {s: 0 for s in self.sem_names}
                for (bsem, bval, _bd) in barrier:
                    if bsem == name or bsem.startswith(name + "_d"):
                        continue
                    if bval > hw[bsem]:
                        eng.wait_ge(sems[bsem], bval)
                        hw[bsem] = bval
                for (kind, fn, deps, slot, ring_wait) in self.steps[name]:
                    for (dsem, dval) in deps:
                        if name == "tensor" and dsem == "tensor":
                            continue
                        if dval > hw[dsem]:
                            eng.wait_ge(sems[dsem], dval)
                            hw[dsem] = dval
                    if kind == "dma":
                        (rsem, rval) = ring_wait
                        if rval > hw[rsem]:
                            eng.wait_ge(sems[rsem], rval)
                            hw[rsem] = rval
                        fn(eng).then_inc(sems[slot], 16)
                    else:
                        fn(eng).then_inc(sems[name], 1)
                for e2 in ENGINES:
                    if e2 != name and final[e2] > 0:
                        eng.wait_ge(sems[e2], final[e2])
                for sname, v in dma_final.items():
                    if not sname.startswith(name) and v > hw.get(sname, 0):
                        eng.wait_ge(sems[sname], v)
            return body

        blk.sync(body_for("sync"))
        blk.tensor(body_for("tensor"))
        blk.vector(body_for("vector"))
        blk.scalar(body_for("scalar"))
        blk.gpsimd(body_for("gpsimd"))


def _flat(x):
    out = []
    for t in x:
        if isinstance(t, list):
            out += _flat(t)
        else:
            out.append(t)
    return out


def build_nc(n_sub=None):
    if n_sub is None:
        n_sub = N_SUBLAYERS
    nc = bass.Bass("TRN2", target_bir_lowering=False)
    lp = nc.allow_low_precision("bf16 operands sanctioned by problem statement")
    lp.__enter__()

    def din(name, shape):
        return nc.dram_tensor(name, list(shape), F32, kind="ExternalInput").ap()

    xT = din("xT", (C, T))
    waT = din("waT", (C, 3 * C))
    wpT = din("wpT", (C, C))
    wq = din("wq", (C, 3 * C))
    wo = din("wo", (C, C))
    rhs = din("rhs", (C, C))
    rhsT = din("rhsT", (C, C))
    delta_pc = din("delta_pc", (128, NP))
    gamma_fm = din("gamma_fm", (128, NCH))
    beta_fm = din("beta_fm", (128, NCH))
    qks = din("qks", (1, 1))
    cosPd = din("cosP", (128, T))
    sinPd = din("sinP", (128, T))
    cosrow = din("cosrow", (1, C))
    msinrow = din("msinrow", (1, C))
    maskTd = din("maskT", (128, 128))
    identd = din("ident", (128, 128))
    sel2d = din("sel2", (2, 128))
    sel2Td = din("sel2T", (128, 2))
    selhd = din("selh", (H, NP * 128))
    onescd = din("onesc", (128, 1))
    onesrd = din("onesr", (1, 128))
    onestd = din("onest", (1, 1))
    constvd = din("constv", (128, 8))

    yout = nc.dram_tensor("y", [C, T], F32, kind="ExternalOutput").ap()
    dbg = {}
    if DBG:
        for nm, shp, dty in [("dbgqs0", [128, T], BF16), ("dbgqs1", [128, T], BF16),
                             ("dbgks0", [128, T], BF16), ("dbgks1", [128, T], BF16),
                             ("dbgvt0", [128, 32 * 80], BF16),
                             ("dbgsb0", [128, T], BF16), ("dbgsb1", [128, T], BF16),
                             ("dbgsb2", [128, T], BF16),
                             ("dbgatt0", [128, 1024], BF16),
                             ("dbgatt1", [128, 1024], BF16),
                             ("dbgoall", [128, T], BF16),
                             ("dbgsums", [16, T], F32)]:
            dbg[nm] = nc.dram_tensor(nm, shp, dty, kind="ExternalOutput").ap()
    Xd0 = nc.dram_tensor("Xd0", [C, T], F32, kind="Internal").ap()
    Xd1 = nc.dram_tensor("Xd1", [C, T], F32, kind="Internal").ap()

    from contextlib import ExitStack
    st = ExitStack()
    e_ = st.enter_context
    with st:
        ACT1 = e_(nc.sbuf_tensor("ACT1", [128, NCH * T], BF16))
        OALL = e_(nc.sbuf_tensor("OALL", [128, NCH * T], BF16))
        CM0 = e_(nc.sbuf_tensor("CM0", [128, T], F32))
        CM1 = e_(nc.sbuf_tensor("CM1", [128, T], F32))
        SB0 = e_(nc.sbuf_tensor("SB0", [128, T], BF16))
        SB1 = e_(nc.sbuf_tensor("SB1", [128, T], BF16))
        SB2 = e_(nc.sbuf_tensor("SB2", [128, T], BF16))
        QS0 = e_(nc.sbuf_tensor("QS0", [128, T], BF16))
        QS1 = e_(nc.sbuf_tensor("QS1", [128, T], BF16))
        KS0 = e_(nc.sbuf_tensor("KS0", [128, T], BF16))
        KS1 = e_(nc.sbuf_tensor("KS1", [128, T], BF16))
        SB3 = QS1
        SB4 = KS1
        VT0 = e_(nc.sbuf_tensor("VT0", [128, 32 * 80], BF16))
        VT1 = e_(nc.sbuf_tensor("VT1", [128, 32 * 80], BF16))
        ATT0 = e_(nc.sbuf_tensor("ATT0", [128, 1024], BF16))
        ATT1 = e_(nc.sbuf_tensor("ATT1", [128, 1024], BF16))
        WA = e_(nc.sbuf_tensor("WA", [128, NCH * 384], BF16))
        WPB0 = WA[:, 0:1024]
        WPB1 = WA[:, 1024:2048]
        QRHS = e_(nc.sbuf_tensor("QRHS", [128, NCH * C], BF16))
        WQV0 = e_(nc.sbuf_tensor("WQV0", [128, NCH * 384], BF16))
        WQV1 = e_(nc.sbuf_tensor("WQV1", [128, NCH * 384], BF16))
        AR1 = e_(nc.sbuf_tensor("AR1", [128, 4096], BF16))
        TCOS = AR1[:, 0:2048]
        TSIN = AR1[:, 2048:4096]
        WSRC = AR1[:, 0:3584]
        SXO0 = e_(nc.sbuf_tensor("SXO0", [128, T], F32))
        SXO1 = e_(nc.sbuf_tensor("SXO1", [128, T], F32))
        SXC = e_(nc.sbuf_tensor("SXC", [128, T], F32))
        AR2 = e_(nc.sbuf_tensor("AR2", [128, T], F32))
        N2 = SXO0[0:34, 0:T]
        SUMS = AR2[0:H, 0:T]
        RSTG = SXO0[:, 0:C]
        CSG = e_(nc.sbuf_tensor("CSG", [128, T], BF16))
        TMSK = e_(nc.sbuf_tensor("TMSK", [128, 128], BF16))
        TID = e_(nc.sbuf_tensor("TID", [128, 128], BF16))
        TIDF = e_(nc.sbuf_tensor("TIDF", [128, 128], F32))
        TR16 = e_(nc.sbuf_tensor("TR16", [16, 128], F32))
        TSEL2 = e_(nc.sbuf_tensor("TSEL2", [2, 128], BF16))
        TSEL2T = e_(nc.sbuf_tensor("TSEL2T", [128, 2], BF16))
        TONESC = e_(nc.sbuf_tensor("TONESC", [128, 1], BF16))
        TONESR = e_(nc.sbuf_tensor("TONESR", [1, 128], BF16))
        TONESRF = e_(nc.sbuf_tensor("TONESRF", [1, 128], F32))
        TONESTF = e_(nc.sbuf_tensor("TONESTF", [1, 1], F32))
        TGAM = e_(nc.sbuf_tensor("TGAM", [128, NCH], F32))
        TBET = e_(nc.sbuf_tensor("TBET", [128, NCH], F32))
        TDELPC = e_(nc.sbuf_tensor("TDELPC", [128, NP], F32))
        TRIGC = e_(nc.sbuf_tensor("TRIGC", [128, NP], F32))
        TRIGS = e_(nc.sbuf_tensor("TRIGS", [128, NP], F32))
        TQKS = e_(nc.sbuf_tensor("TQKS", [1, 1], F32))
        TCONST = e_(nc.sbuf_tensor("TCONST", [128, 8], F32))
        QKCOL = e_(nc.sbuf_tensor("QKCOL", [128, 1], F32))
        SCCOL = e_(nc.sbuf_tensor("SCCOL", [128, NCH], F32))
        CORRC = e_(nc.sbuf_tensor("CORRC", [128, NCH], F32))
        BIASC = e_(nc.sbuf_tensor("BIASC", [128, 3 * NCH], F32))
        BGCOL = e_(nc.sbuf_tensor("BGCOL", [128, NCH], BF16))
        AMX = e_(nc.sbuf_tensor("AMX", [128, 16], F32))
        AMXS = e_(nc.sbuf_tensor("AMXS", [128, 16], F32))
        CROWC = e_(nc.sbuf_tensor("CROWC", [1, C], BF16))
        CROWM = e_(nc.sbuf_tensor("CROWM", [1, C], BF16))
        SROW = SXC[0:1, 0:T]
        SROW2 = CM1[0:1, 0:T]
        BRH = CM0[0:1, 1024:2048]
        TSELHF = CM1[0:H, 0:1024]

        PQ = e_(nc.psum_tensor("PQ", [128, 2048], F32))
        PS1 = e_(nc.psum_tensor("PS1", [128, 1024], F32))
        PS2 = e_(nc.psum_tensor("PS2", [128, 1024], F32))

        S = Sched()
        sems = {n: e_(nc.semaphore("sem_" + n)) for n in S.sem_names}
        blk = e_(nc.Block())

        def gload(dst, src, deps=()):
            return S.step("gpsimd", lambda e, d=dst, s=src: e.dma_start(out=d, in_=s),
                          deps=deps, dma=True)

        def sdma(dst, src, deps=()):
            return S.step("sync", lambda e, d=dst, s=src: e.dma_start(out=d, in_=s),
                          deps=deps, dma=True)

        t_cosP = gload(TCOS[:, :], cosPd)
        t_sinP = gload(TSIN[:, :], sinPd)
        t_msk = gload(TMSK[:, :], maskTd)
        t_id = gload(TID[:, :], identd)
        t_idf = sdma(TIDF[:, :], identd)
        t_sel2 = gload(TSEL2[:, :], sel2d)
        t_sel2t = gload(TSEL2T[:, :], sel2Td)

        t_onesc = gload(TONESC[:, :], onescd)
        t_onesr = gload(TONESR[:, :], onesrd)
        t_crc = gload(CROWC[:, :], cosrow)
        t_crm = gload(CROWM[:, :], msinrow)
        t_onesrf = sdma(TONESRF[:, :], onesrd)
        t_onestf = sdma(TONESTF[:, :], onestd)
        t_gam = sdma(TGAM[:, :], gamma_fm)
        t_bet = sdma(TBET[:, :], beta_fm)
        t_del = sdma(TDELPC[:, :], delta_pc)
        t_qks = sdma(TQKS[:, :], qks)
        t_const = sdma(TCONST[:, :], constvd)
        for i, v in enumerate([0.5, 1.0, 1e-5, 0.125, -0.5, -1.0, 0.0, 2.0]):
            nc.const_aps.register(F32, v, TCONST[:, i:i + 1])
        S.setup_barrier = [t_const]

        t_vt0 = S.step("vector", lambda e: e.memset(VT0[:, :], 1.0))
        t_vt1 = S.step("vector", lambda e: e.memset(VT1[:, :], 1.0))

        t = S.step("tensor", lambda e: e.matmul(PS2[:, 0:1], TONESRF[:, :], TQKS[:, :],
                                                start=True, stop=True),
                   deps=[t_onesrf, t_qks])
        t_qkcol = S.step("vector", lambda e: e.tensor_copy(out=QKCOL[:, :],
                                                           in_=PS2[:, 0:1]), deps=[t])

        PI_HI = 3.140625
        PI_LO = float(np.float32(math.pi - PI_HI))
        TWO_PI = float(np.float32(2 * math.pi))
        HALF_PI = float(np.float32(math.pi / 2))
        dcl = CM0[:, 0:NP]
        drr = CM0[:, NP:2 * NP]
        ww = CM0[:, 2 * NP:3 * NP]
        sdr = CM0[:, 3 * NP:4 * NP]
        cdr = CM0[:, 4 * NP:5 * NP]
        t = S.step("vector", lambda e: e.tensor_scalar(out=dcl, in0=TDELPC[:, :],
                                                       scalar1=-TWO_PI, scalar2=0.0,
                                                       op0=ALU.max, op1=ALU.min),
                   deps=[t_del])
        t_dr = S.step("vector", lambda e: e.tensor_scalar(out=drr, in0=dcl,
                                                          scalar1=PI_HI, scalar2=PI_LO,
                                                          op0=ALU.add, op1=ALU.add),
                      deps=[t])
        t_sd = S.step("scalar", lambda e: e.activation(sdr, drr, AF.Sin), deps=[t_dr])
        TP_HI = 6.28125
        TP_LO = float(np.float32(2 * math.pi - TP_HI))
        HP_HI = 1.5703125
        HP_LO = float(np.float32(math.pi / 2 - HP_HI))
        sel = CM0[:, 5 * NP:6 * NP]
        t = S.step("vector", lambda e: e.tensor_scalar(out=sel, in0=drr,
                                                       scalar1=HALF_PI, scalar2=None,
                                                       op0=ALU.is_gt), deps=[t_dr])
        t = S.step("vector", lambda e: e.scalar_tensor_tensor(out=ww, in0=sel,
                                                              scalar=-TP_HI, in1=drr,
                                                              op0=ALU.mult, op1=ALU.add),
                   deps=[t])
        t = S.step("vector", lambda e: e.scalar_tensor_tensor(out=ww, in0=sel,
                                                              scalar=-TP_LO, in1=ww,
                                                              op0=ALU.mult, op1=ALU.add),
                   deps=[t])
        t = S.step("vector", lambda e: e.tensor_scalar(out=ww, in0=ww, scalar1=HP_HI,
                                                       scalar2=HP_LO, op0=ALU.add,
                                                       op1=ALU.add), deps=[t])
        t_cd = S.step("scalar", lambda e: e.activation(cdr, ww, AF.Sin),
                      deps=[t, t_sd])
        t_tc = S.step("vector", lambda e: e.tensor_scalar(out=TRIGC[:, :], in0=cdr,
                                                          scalar1=-1.0, scalar2=None,
                                                          op0=ALU.mult), deps=[t_cd])
        t_ts = S.step("vector", lambda e: e.tensor_scalar(out=TRIGS[:, :], in0=sdr,
                                                          scalar1=-1.0, scalar2=None,
                                                          op0=ALU.mult),
                      deps=[t_cd, t_tc])
        t_trig = [t_tc, t_ts]


        def row_from_cols(dst_row, src_cols, n, deps):
            tt_ = S.step("tensor", lambda e, n=n: e.transpose(
                PS2[0:n, 0:128], src_cols, TIDF[:, :]),
                deps=list(deps) + [t_idf])
            tv = S.step("vector", lambda e, n=n: e.tensor_copy(
                out=TR16[0:n, 0:128], in_=PS2[0:n, 0:128]), deps=[tt_])
            return sdma(dst_row, TR16[0:n, 0:128], deps=[tv])

        def recip_chain(dst, src, deps, scale=-1.0, bias=0.0):
            t1 = S.step("scalar", lambda e: e.activation(dst, src, AF.Ln, bias=bias),
                        deps=deps)
            return S.step("scalar", lambda e: e.activation(dst, dst, AF.Exp,
                                                           scale=scale), deps=[t1])

        def emit_cascade(t1buf, w, out_ap, deps, mbufs=None, prev_st=None):
            m = mbufs if mbufs is not None else [CM1, SXC]
            h = w // 2
            halves = [(0, m[0]), (h, m[1])]
            st = {0: None, 1: None}
            Ccum = 0.0
            for lv, tt in enumerate(CAS_T):
                tau = tt - 128.5
                bias = -(Ccum + tau)
                for hi, (o, mb) in enumerate(halves):
                    if lv == 0:
                        tsg = S.step("scalar", lambda e, bias=bias, o=o, h=h:
                                     e.activation(CSG[:, o:o + h],
                                                  t1buf[:, o:o + h],
                                                  AF.Sign, bias=bias),
                                     deps=list(deps) +
                                     ([prev_st[hi]] if prev_st else []))
                        st[hi] = S.step("vector", lambda e, tt=tt, mb=mb, o=o,
                                        h=h: e.scalar_tensor_tensor(
                            out=mb[:, 0:h], in0=CSG[:, o:o + h],
                            scalar=-tt / 2.0, in1=t1buf[:, o:o + h],
                            op0=ALU.mult, op1=ALU.add), deps=[tsg])
                    else:
                        tsg = S.step("scalar", lambda e, bias=bias, mb=mb, o=o,
                                     h=h: e.activation(CSG[:, o:o + h],
                                                       mb[:, 0:h],
                                                       AF.Sign, bias=bias),
                                     deps=[st[hi]])
                        st[hi] = S.step("vector", lambda e, tt=tt, mb=mb, o=o,
                                        h=h: e.scalar_tensor_tensor(
                            out=mb[:, 0:h], in0=CSG[:, o:o + h],
                            scalar=-tt / 2.0, in1=mb[:, 0:h],
                            op0=ALU.mult, op1=ALU.add), deps=[tsg])
                Ccum += tt / 2.0
            out_tok = None
            for hi, (o, mb) in enumerate(halves):
                out_tok = S.step("vector", lambda e, mb=mb, o=o, h=h:
                                 e.scalar_tensor_tensor(
                    out=out_ap[:, o:o + h], in0=mb[:, 0:h], scalar=-1.0,
                    in1=t1buf[:, o:o + h], op0=ALU.mult, op1=ALU.add),
                    deps=[st[hi]])
            return out_tok, dict(st)

        def emit_ln(src_hbm, phase_deps, gamma_mode=False, apply=True):
            pd = _flat(phase_deps)
            per_chunk = (len(pd) == NCH and
                         all(t[0].startswith("sync") for t in pd))
            ltoks = []
            for cc in range(NCH):
                if per_chunk:
                    d = [pd[cc]]
                elif cc == 0:
                    d = pd
                else:
                    d = ()
                ltoks.append(gload(ACT1[:, cc * T:(cc + 1) * T],
                                   src_hbm[cc * 128:(cc + 1) * 128, :], deps=d))
            sq_toks = []
            for cc in range(NCH):
                sq_toks.append(S.step("vector", lambda e, cc=cc: e.tensor_mul(
                    out=OALL[:, cc * T:(cc + 1) * T],
                    in0=ACT1[:, cc * T:(cc + 1) * T],
                    in1=ACT1[:, cc * T:(cc + 1) * T]),
                    deps=[ltoks[cc]] + (list(phase_deps) if cc == 0 else [])))
            t_sq = sq_toks[-1]
            ev = []
            for g in range(4):
                d1 = (ltoks + [t_onesc] + list(phase_deps)) if g == 0 \
                    else [ev[g - 1][0]]
                tk = None
                for cc in range(NCH):
                    tk = S.step("tensor", lambda e, g=g, cc=cc: e.matmul(
                        PS1[0:1, 512 * (g % 2):512 * (g % 2) + 512], TONESC[:, :],
                        ACT1[:, cc * T + 512 * g:cc * T + 512 * g + 512],
                        start=(cc == 0), stop=(cc == NCH - 1)),
                        deps=(d1 if cc == 0 else []))
                tk2 = None
                for cc in range(NCH):
                    if g == 0:
                        d2 = [sq_toks[cc]]
                    else:
                        d2 = [ev[g - 1][1]] if cc == 0 else []
                    tk2 = S.step("tensor", lambda e, g=g, cc=cc: e.matmul(
                        PS2[0:1, 512 * (g % 2):512 * (g % 2) + 512], TONESC[:, :],
                        OALL[:, cc * T + 512 * g:cc * T + 512 * g + 512],
                        start=(cc == 0), stop=(cc == NCH - 1)), deps=d2)
                e1 = S.step("scalar", lambda e, g=g: e.copy(
                    SROW[0:1, 512 * g:512 * g + 512],
                    PS1[0:1, 512 * (g % 2):512 * (g % 2) + 512]), deps=[tk])
                e2 = S.step("scalar", lambda e, g=g: e.copy(
                    SROW2[0:1, 512 * g:512 * g + 512],
                    PS2[0:1, 512 * (g % 2):512 * (g % 2) + 512]), deps=[tk2])
                ev.append((e1, e2))
            bt = []
            v2_of_bank = {}
            for g in range(4):
                e1, e2 = ev[g]
                b1 = S.step("tensor", lambda e, g=g: e.matmul(
                    PQ[:, 512 * g:512 * g + 512], TONESRF[:, :],
                    SROW[0:1, 512 * g:512 * g + 512], start=True, stop=True),
                    deps=[e1, t_onesrf])
                v1 = S.step("vector", lambda e, g=g: e.tensor_scalar(
                    out=CM0[:, 512 * g:512 * g + 512], in0=PQ[:, 512 * g:512 * g + 512],
                    scalar1=1.0 / C, scalar2=None, op0=ALU.mult), deps=[b1])
                b2 = S.step("tensor", lambda e, g=g: e.matmul(
                    PS1[:, 512 * (g % 2):512 * (g % 2) + 512], TONESRF[:, :],
                    SROW2[0:1, 512 * g:512 * g + 512], start=True, stop=True),
                    deps=[e2] + ([ev[3][0], ev[3][1]] if g == 0 else []) +
                    ([v2_of_bank[g % 2]] if (g % 2) in v2_of_bank else []))
                v2 = S.step("vector", lambda e, g=g: e.tensor_scalar(
                    out=CM1[:, 512 * g:512 * g + 512],
                    in0=PS1[:, 512 * (g % 2):512 * (g % 2) + 512],
                    scalar1=1.0 / C, scalar2=None, op0=ALU.mult), deps=[b2])
                v2_of_bank[g % 2] = v2
                bt += [v1, v2]
            t_m2 = S.step("scalar", lambda e: e.activation(SXC[:, :], CM0[:, :],
                                                           AF.Square), deps=bt)
            t_var = S.step("vector", lambda e: e.tensor_sub(out=CM1[:, :], in0=CM1[:, :],
                                                            in1=SXC[:, :]), deps=[t_m2])
            if not gamma_mode:
                t_r = recip_chain(SXC[:, :], CM1[:, :], [t_var], scale=-0.5,
                                  bias=LN_EPS)
            else:
                t1 = S.step("scalar", lambda e: e.activation(SXC[:, :], CM1[:, :],
                                                             AF.Ln, bias=LN_EPS),
                            deps=[t_var])
                t2 = S.step("scalar", lambda e: e.activation(SXO1[:, :], SXC[:, :],
                                                             AF.Exp, scale=-1.0),
                            deps=[t1])
                t3 = S.step("vector", lambda e: e.tensor_mul(out=SXO1[:, :],
                                                             in0=SXO1[:, :],
                                                             in1=CM1[:, :]), deps=[t2])
                t4 = S.step("scalar", lambda e: e.activation(SXO1[:, :], SXO1[:, :],
                                                             AF.Ln, bias=LN_EPS),
                            deps=[t3])
                t5 = S.step("vector", lambda e: e.tensor_add(out=SXC[:, :],
                                                             in0=SXC[:, :],
                                                             in1=SXO1[:, :]), deps=[t4])
                t_r = S.step("scalar", lambda e: e.activation(SXC[:, :], SXC[:, :],
                                                              AF.Exp, scale=-0.5),
                             deps=[t5])
            if not apply:
                return [t_r]
            t_rb = S.step("vector", lambda e: e.tensor_copy(out=SB3[:, :],
                                                            in_=SXC[:, :]), deps=[t_r])
            t_bb = S.step("vector", lambda e: e.scalar_tensor_tensor(
                out=SB4[:, :], in0=CM0[:, :], scalar=-1.0, in1=SXC[:, :],
                op0=ALU.mult, op1=ALU.mult), deps=[t_r])
            atoks = []
            prevc = None
            for cc in range(NCH):
                ta1 = S.step("vector", lambda e, cc=cc: e.tensor_mul(
                    out=ACT1[:, cc * T:(cc + 1) * T], in0=ACT1[:, cc * T:(cc + 1) * T],
                    in1=SB3[:, :]),
                    deps=[t_rb, sq_toks[cc]] + ([prevc] if prevc else []))
                tb2 = S.step("vector", lambda e, cc=cc: e.tensor_add(
                    out=ACT1[:, cc * T:(cc + 1) * T], in0=ACT1[:, cc * T:(cc + 1) * T],
                    in1=SB4[:, :]), deps=[t_bb, ta1])
                prevc = tb2
                atoks.append(tb2)
            return atoks

        def emit_attention(layer, prep_pair, pair_end, exp_scale_imm, exp_scale_ap,
                           dep_start, body_maker=None, aux=None):
            aux = aux if aux is not None else []
            evacs = []
            prev = list(dep_start)
            S.attn_sums_gate = None
            exp_of = {0: None, 1: None}
            av_of = {0: None, 1: None}
            bank_gate = {0: None, 1: None}
            sums_gate = [None]
            nthunks = []
            fin_next = [None]
            if body_maker is not None:
                th0, fin0 = body_maker(0, {"start": dep_start,
                                           "rope": list(dep_start),
                                           "pq23": None, "sb2": [], "vt": []})
                for t in th0:
                    t()
                fin_next[0] = fin0
            for p8 in range(NP):
                if body_maker is None:
                    tprep, qs_t, ks_t, vt, vt_tok = prep_pair(p8, prev)
                else:
                    (tprep, qs_t, ks_t, vt, vt_tok, ropetoks, pq23tok,
                     lastmm) = fin_next[0](prev)
                    if p8 + 1 < NP:
                        nthunks, fin_next[0] = body_maker(
                            p8 + 1, {"start": [], "rope": ropetoks,
                                     "pq23": pq23tok, "sb2": list(vt_tok),
                                     "vt": [S.tok("tensor")],
                                     "wq_free": lastmm})
                    else:
                        nthunks = []
                if DBG and layer == "a" and p8 == 0:
                    tp0 = (tprep[1] if isinstance(tprep[0], tuple)
                           else tprep)
                    dl = [sdma(dbg["dbgqs0"], QS0[:, :], deps=[tp0]),
                          sdma(dbg["dbgqs1"], QS1[:, :], deps=[tp0]),
                          sdma(dbg["dbgks0"], KS0[:, :], deps=[tp0]),
                          sdma(dbg["dbgks1"], KS1[:, :], deps=[tp0]),
                          sdma(dbg["dbgvt0"], vt[:, :], deps=[tp0]),
                          sdma(dbg["dbgsb0"], SB0[:, :], deps=[tp0]),
                          sdma(dbg["dbgsb1"], SB1[:, :], deps=[tp0]),
                          sdma(dbg["dbgsb2"], SB2[:, :], deps=[tp0])]
                    S.step("vector", lambda e: e.engine_nop(), deps=dl)
                for hh in range(2):
                    h = 2 * p8 + hh
                    QSt, KSt = qs_t[hh], ks_t[hh]
                    lo, hi = 0, 128
                    pairs = []
                    jg = 0
                    for g in range(4):
                        njs = 4 * g + 4
                        for j0 in range(0, njs, 2):
                            pairs.append((g, [j0, j0 + 1], jg % 2, njs))
                            jg += 1
                    ev_g = {}
                    sr_g = {}

                    def emit_avs(pend, vt=vt, hh=hh, h=h):
                        g, jp, sb, njs, ex, ATT = pend
                        bk = g % 2
                        last_av = None
                        for ji, j in enumerate(jp):
                            off = max(0, 128 * j - 512 * g)
                            n = 512 - off
                            first = (j == 0)
                            last = (j == njs - 1)
                            d = [ex]
                            if first and g == 0:
                                d += list(vt_tok)
                            if first and bank_gate[bk] is not None:
                                d += bank_gate[bk]
                            last_av = S.step("tensor", lambda e, j=j, ji=ji,
                                             off=off, n=n, bk=bk, first=first,
                                             last=last, ATT=ATT, vt=vt, hh=hh:
                                             e.matmul(
                                PQ[0:65, 512 * bk + off:512 * bk + off + n],
                                vt[:, (16 * hh + j) * 80:(16 * hh + j) * 80 + 65],
                                ATT[:, 512 * ji + off:512 * ji + off + n],
                                start=first, stop=last), deps=d)
                        av_of[sb] = last_av
                        if jp[-1] == njs - 1:
                            g_ = g
                            ev_g[g_] = S.step("vector", lambda e, g=g_, bk=bk,
                                              h=h: e.tensor_copy(
                                out=OALL[64 * (h % 2):64 * (h % 2) + 64,
                                         (h // 2) * T + 512 * g:
                                         (h // 2) * T + 512 * g + 512],
                                in_=PQ[0:64, 512 * bk:512 * bk + 512]),
                                deps=[last_av])
                            sr_g[g_] = S.step("vector", lambda e, g=g_, bk=bk:
                                              e.tensor_copy(
                                out=SROW[0:1, 512 * g:512 * g + 512],
                                in_=PQ[64:65, 512 * bk:512 * bk + 512]),
                                deps=[last_av] +
                                ([sums_gate[0]] if (g_ == 0 and sums_gate[0])
                                 else []))
                            bank_gate[bk] = [ev_g[g_], sr_g[g_]]

                    pend = None
                    for pi, (g, jp, sb, njs) in enumerate(pairs):
                        SC = PS1 if sb == 0 else PS2
                        ATT = ATT0 if sb == 0 else ATT1
                        mm_tok = None
                        for ji, j in enumerate(jp):
                            off = max(0, 128 * j - 512 * g)
                            n = 512 - off
                            qlo = 512 * g + off
                            diag = (j >= 4 * g)
                            d = []
                            if ji == 0:
                                d = [tprep[hh]
                                     if isinstance(tprep[0], tuple)
                                     else tprep]
                                if exp_of[sb] is not None:
                                    d.append(exp_of[sb])
                            mm_tok = S.step("tensor", lambda e, j=j, ji=ji, off=off,
                                            n=n, qlo=qlo, diag=diag, SC=SC, QSt=QSt,
                                            KSt=KSt, lo=lo, hi=hi: e.matmul(
                                SC[:, 512 * ji + off:512 * ji + off + n],
                                KSt[lo:hi, 128 * j:128 * j + 128],
                                QSt[lo:hi, qlo:qlo + n],
                                start=True, stop=(not diag)), deps=d)
                            if diag:
                                mm_tok = S.step("tensor", lambda e, ji=ji, off=off,
                                                SC=SC: e.matmul(
                                    SC[:, 512 * ji + off:512 * ji + off + 128],
                                    TID[:, :], TMSK[:, :], start=False, stop=True),
                                    deps=[])
                        ed = [mm_tok]
                        off0 = max(0, 128 * jp[0] - 512 * g)
                        if exp_scale_ap is not None:
                            ex = S.step("scalar", lambda e, SC=SC, ATT=ATT,
                                        off0=off0:
                                        e.activation(ATT[:, off0:1024],
                                                     SC[:, off0:1024], AF.Exp,
                                                     scale=exp_scale_ap),
                                        deps=ed)
                        else:
                            ex = S.step("scalar", lambda e, SC=SC, ATT=ATT,
                                        off0=off0:
                                        e.activation(ATT[:, off0:1024],
                                                     SC[:, off0:1024], AF.Exp,
                                                     scale=exp_scale_imm),
                                        deps=ed)
                        exp_of[sb] = ex
                        if (DBG and layer == "a" and p8 == 0 and hh == 0
                                and pi in (0, 1)):
                            exp_of[sb] = sdma(dbg["dbgatt%d" % pi],
                                              ATT[:, :], deps=[ex])
                        if pend is not None:
                            emit_avs(pend)
                            if nthunks:
                                nthunks.pop(0)()
                            elif aux:
                                aux.pop(0)()
                        pend = (g, jp, sb, njs, ex, ATT)
                    emit_avs(pend)
                    while nthunks:
                        nthunks.pop(0)()
                    tpl = S.step("sync", lambda e, h=h: e.dma_start(
                        out=SUMS[h:h + 1, :], in_=SROW[0:1, :]), dma=True,
                        deps=[sr_g[gg] for gg in range(4)])
                    sums_gate[0] = tpl
                    S.attn_sums_gate = tpl
                    evacs.append((ev_g[3], tpl))
                    if DBG and layer == "a" and p8 == 0 and hh == 1:
                        dob = sdma(dbg["dbgoall"], OALL[:, 0:T],
                                   deps=[ev_g[3], evacs[-2][0]])
                        evacs.append((dob, dob))
                if pair_end is not None:
                    pair_end(p8)
                prev = [S.tok("tensor"), S.tok("scalar"), S.tok("vector")]
            while aux:
                aux.pop(0)()
            return evacs

        def emit_norm_oall(evacs):
            sum_deps = [e[1] for e in evacs] + [e[0] for e in evacs] + \
                [S.tok("vector"), S.tok("tensor")]
            t_r = recip_chain(SUMS[:, :], SUMS[:, :], sum_deps, scale=-1.0)
            t_lh = sdma(TSELHF[:, :], selhd, deps=sum_deps)
            toks = []
            if DBG:
                toks.append(sdma(dbg["dbgsums"], SUMS[:, :], deps=[t_r]))
            for p8 in range(NP):
                bs = []
                for g in range(4):
                    d = [t_r, t_lh] + [e[0] for e in evacs] if (p8 == 0 and g == 0) \
                        else ([toks[-1]] if (g == 0 and toks) else [])
                    bs.append(S.step("tensor", lambda e, p8=p8, g=g: e.matmul(
                        PQ[:, 512 * g:512 * g + 512],
                        TSELHF[:, p8 * 128:(p8 + 1) * 128],
                        SUMS[:, 512 * g:512 * g + 512],
                        start=True, stop=True), deps=d))
                tm = S.step("vector", lambda e, p8=p8: e.tensor_mul(
                    out=OALL[:, p8 * T:(p8 + 1) * T], in0=OALL[:, p8 * T:(p8 + 1) * T],
                    in1=PQ[:, 0:2048]), deps=[bs[-1], bs[0]])
                toks.append(tm)
            return toks

        def emit_proj(w_hbm, src_hbm, dst_hbm, phase_deps):
            wtok = [None, None]

            def loadw(oc, deps):
                b = WPB0 if oc % 2 == 0 else WPB1
                tks = []
                for cc in range(NCH):
                    tks.append(gload(
                        b[:, cc * 128:(cc + 1) * 128],
                        w_hbm[cc * 128:(cc + 1) * 128, oc * 128:(oc + 1) * 128],
                        deps=(deps if cc == 0 else ())))
                wtok[oc % 2] = tks

            loadw(0, phase_deps)
            loadw(1, phase_deps)
            evac_tok = [None, None]
            last_store = None
            st_toks = []
            for oc in range(NCH):
                b = WPB0 if oc % 2 == 0 else WPB1
                mm = None
                for tg in range(4):
                    for cc in range(NCH):
                        d = []
                        if tg == 0 and cc == 0:
                            d = [wtok[oc % 2]] + list(phase_deps)
                            if evac_tok[oc % 2] is not None:
                                d.append(evac_tok[oc % 2])
                        if oc % 2 == 0:
                            mm = S.step("tensor", lambda e, tg=tg, cc=cc, b=b: e.matmul(
                                PQ[:, 512 * tg:512 * tg + 512],
                                b[:, cc * 128:(cc + 1) * 128],
                                OALL[:, cc * T + 512 * tg:cc * T + 512 * tg + 512],
                                start=(cc == 0), stop=(cc == NCH - 1)), deps=d)
                        else:
                            PSx = PS1 if tg < 2 else PS2
                            mm = S.step("tensor", lambda e, tg=tg, cc=cc, b=b, PSx=PSx:
                                        e.matmul(
                                PSx[:, 512 * (tg % 2):512 * (tg % 2) + 512],
                                b[:, cc * 128:(cc + 1) * 128],
                                OALL[:, cc * T + 512 * tg:cc * T + 512 * tg + 512],
                                start=(cc == 0), stop=(cc == NCH - 1)), deps=d)
                if oc + 2 < NCH:
                    loadw(oc + 2, [mm])
                xbuf = SXO0 if oc % 2 == 0 else SXO1
                t_old = sdma(xbuf[:, :], src_hbm[oc * 128:(oc + 1) * 128, :],
                             deps=([evac_tok[oc % 2]] if evac_tok[oc % 2] else
                                   list(phase_deps)))
                if oc % 2 == 0:
                    t_add = S.step("vector", lambda e, xbuf=xbuf: e.tensor_add(
                        out=SXC[:, :], in0=PQ[:, 0:2048], in1=xbuf[:, :]),
                        deps=[mm, t_old] + ([last_store] if last_store else []))
                else:
                    ta1 = S.step("vector", lambda e, xbuf=xbuf: e.tensor_add(
                        out=SXC[:, 0:1024], in0=PS1[:, :], in1=xbuf[:, 0:1024]),
                        deps=[mm, t_old] + ([last_store] if last_store else []))
                    t_add = S.step("vector", lambda e, xbuf=xbuf: e.tensor_add(
                        out=SXC[:, 1024:2048], in0=PS2[:, :], in1=xbuf[:, 1024:2048]),
                        deps=[ta1])
                evac_tok[oc % 2] = t_add
                t_st = S.step("sync", lambda e, oc=oc: e.dma_start(
                    out=dst_hbm[oc * 128:(oc + 1) * 128, :], in_=SXC[:, :]),
                    dma=True, deps=[t_add])
                last_store = t_st
                st_toks.append(t_st)
            return st_toks

        def emit_rhs_prep_scales(deps):
            am_toks = []
            for oc in range(NCH):
                tl = sdma(RSTG[:, :], rhsT[oc * 128:(oc + 1) * 128, :],
                          deps=(list(deps) if oc == 0 else [am_toks[-1]]))
                tr = S.step("vector", lambda e, oc=oc: e.tensor_reduce(
                    out=AMX[:, oc:oc + 1], in_=RSTG[:, :], axis=AX.X, op=ALU.max,
                    apply_absolute_value=True), deps=[tl])
                am_toks.append(tr)
            t0 = S.step("vector", lambda e: e.tensor_scalar(
                out=AMX[:, 0:NCH], in0=AMX[:, 0:NCH], scalar1=1e-12, scalar2=None,
                op0=ALU.max), deps=am_toks)
            t1 = S.step("vector", lambda e: e.tensor_scalar(
                out=SCCOL[:, :], in0=AMX[:, 0:NCH], scalar1=1.0 / 127.0, scalar2=None,
                op0=ALU.mult), deps=[t0])
            t2 = recip_chain(AMXS[:, 0:NCH], AMX[:, 0:NCH], [t1], scale=-1.0)
            t3 = S.step("vector", lambda e: e.tensor_scalar(
                out=AMXS[:, 0:NCH], in0=AMXS[:, 0:NCH], scalar1=127.0, scalar2=None,
                op0=ALU.mult), deps=[t2])
            rowd = row_from_cols(SXO1[0:1, 0:C], AMXS[:, 0:NCH], NCH, [t3])
            bca = []
            for g in range(2):
                b1 = S.step("tensor", lambda e, g=g: e.matmul(
                    PS1[:, 512 * g:512 * g + 512], TONESRF[:, :],
                    SXO1[0:1, 512 * g:512 * g + 512],
                    start=True, stop=True), deps=[rowd, t_onesrf])
                bca.append(S.step("vector", lambda e, g=g: e.tensor_copy(
                    out=SXO1[:, 512 * g:512 * g + 512],
                    in_=PS1[:, 512 * g:512 * g + 512]), deps=[b1]))
            return [bca[-1], t1], am_toks[-1]

        rhs_qs = {"qtok": None}

        def make_rhs_q_thunks(sc_toks, am_last):
            thunks = []
            ctx = {"tm_prev": None, "tq_prev": {}, "pst": None}
            MB = [SXO1[:, 1024:1536], SXO1[:, 1536:2048]]
            for cc in range(NCH):
                t1b = SXO0[:, 0:1024] if cc % 2 == 0 else SXO0[:, 1024:2048]

                def th_lm(cc=cc, t1b=t1b):
                    dld = [am_last] if cc == 0 else []
                    if (cc % 2) in ctx["tq_prev"]:
                        dld.append(ctx["tq_prev"][cc % 2])
                    tl = sdma(t1b[:, 0:C], rhs[cc * 128:(cc + 1) * 128, :],
                              deps=dld)
                    ctx["tm_prev"] = S.step("vector", lambda e, t1b=t1b:
                                            e.tensor_mul(
                        out=t1b[:, 0:C], in0=t1b[:, 0:C],
                        in1=SXO1[:, 0:C]),
                        deps=[tl] + list(sc_toks))
                    ctx["lvst"] = {0: None, 1: None}
                    ctx["Ccum"] = 0.0
                thunks.append(th_lm)

                def th_lv(lv0, cc=cc, t1b=t1b):
                    def f():
                        h = 512
                        for lv in (lv0, lv0 + 1):
                            tt = CAS_T[lv]
                            bias = -(ctx["Ccum"] + tt - 128.5)
                            for hi in range(2):
                                o = hi * h
                                mb = MB[hi]
                                if lv == 0:
                                    tsg = S.step("scalar",
                                                 lambda e, bias=bias, o=o:
                                                 e.activation(
                                        CSG[:, o:o + h], t1b[:, o:o + h],
                                        AF.Sign, bias=bias),
                                        deps=[ctx["tm_prev"]] +
                                        ([ctx["pst"][hi]] if ctx["pst"]
                                         else []))
                                    ctx["lvst"][hi] = S.step(
                                        "vector", lambda e, tt=tt, mb=mb, o=o:
                                        e.scalar_tensor_tensor(
                                            out=mb[:, 0:h],
                                            in0=CSG[:, o:o + h],
                                            scalar=-tt / 2.0,
                                            in1=t1b[:, o:o + h],
                                            op0=ALU.mult, op1=ALU.add),
                                        deps=[tsg])
                                else:
                                    tsg = S.step("scalar",
                                                 lambda e, bias=bias, o=o,
                                                 mb=mb: e.activation(
                                        CSG[:, o:o + h], mb[:, 0:h],
                                        AF.Sign, bias=bias),
                                        deps=[ctx["lvst"][hi]])
                                    ctx["lvst"][hi] = S.step(
                                        "vector", lambda e, tt=tt, mb=mb, o=o:
                                        e.scalar_tensor_tensor(
                                            out=mb[:, 0:h],
                                            in0=CSG[:, o:o + h],
                                            scalar=-tt / 2.0,
                                            in1=mb[:, 0:h],
                                            op0=ALU.mult, op1=ALU.add),
                                        deps=[tsg])
                            ctx["Ccum"] += tt / 2.0
                    return f
                for lv0 in (0, 2, 4, 6):
                    thunks.append(th_lv(lv0))

                def th_fin(cc=cc, t1b=t1b):
                    ot = None
                    for hi in range(2):
                        o = hi * 512
                        mb = MB[hi]
                        ot = S.step("vector", lambda e, mb=mb, o=o, cc=cc:
                                    e.scalar_tensor_tensor(
                            out=QRHS[:, cc * C + o:cc * C + o + 512],
                            in0=mb[:, 0:512], scalar=-1.0,
                            in1=t1b[:, o:o + 512],
                            op0=ALU.mult, op1=ALU.add),
                            deps=[ctx["lvst"][hi]])
                    ctx["pst"] = dict(ctx["lvst"])
                    tq = S.step("vector", lambda e, cc=cc: e.tensor_scalar(
                        out=QRHS[:, cc * C:(cc + 1) * C],
                        in0=QRHS[:, cc * C:(cc + 1) * C],
                        scalar1=-0.5, scalar2=None, op0=ALU.add), deps=[ot])
                    ctx["tq_prev"][cc % 2] = tq
                    rhs_qs["qtok"] = tq
                thunks.append(th_fin)
            return thunks

        def emit_rhs_colsums(deps):
            cs_last = None
            for half in range(2):
                for cc in range(NCH):
                    cs_last = S.step("tensor", lambda e, cc=cc, half=half: e.matmul(
                        PS1[0:1, 512 * half:512 * half + 512], TONESC[:, :],
                        QRHS[:, cc * C + 512 * half:cc * C + 512 * half + 512],
                        start=(cc == 0), stop=(cc == NCH - 1)),
                        deps=([rhs_qs["qtok"], t_onesc] + list(deps)
                              if (cc == 0 and half == 0) else []))
            tev = S.step("vector", lambda e: e.tensor_copy(out=BRH[0:1, 0:C],
                                                           in_=PS1[0:1, 0:1024]),
                         deps=[cs_last])
            ctoks = []
            for oc in range(NCH):
                tc1 = S.step("tensor", lambda e, oc=oc: e.matmul(
                    PS2[:, 0:1], BRH[0:1, oc * 128:(oc + 1) * 128], TONESTF[:, :],
                    start=True, stop=True),
                    deps=[tev, t_onestf] + ([ctoks[-1]] if ctoks else []))
                ctoks.append(S.step("vector", lambda e, oc=oc: e.tensor_scalar(
                    out=CORRC[:, oc:oc + 1], in0=PS2[:, 0:1],
                    scalar1=-0.5, scalar2=None, op0=ALU.mult), deps=[tc1]))
            return [rhs_qs["qtok"], ctoks[-1]]

        def emit_ffn(src_hbm, dst_hbm, ln_toks, rhs_ready):
            tmx = S.step("vector", lambda e: e.memset(AMX[:, :], 0.0),
                         deps=list(ln_toks))
            tw = None
            prev_ttr = None
            prev_trd = None
            for cc in range(NCH):
                tld = sdma(SXO0[:, :], src_hbm[cc * 128:(cc + 1) * 128, :],
                           deps=(list(ln_toks) if cc == 0 else [tw]))
                tw = S.step("vector", lambda e: e.tensor_sub(
                    out=CM1[:, :], in0=SXO0[:, :], in1=CM0[:, :]),
                    deps=[tld] + (list(ln_toks) if cc == 0 else [prev_ttr]))
                pq3 = PQ[:, 0:2048].rearrange(
                    "p (b n) -> p b n", b=4)[:, :, 0:128]
                trds = []
                for r_ in range(4):
                    for b in range(4):
                        tb_ = 4 * r_ + b
                        d = []
                        if b == 0:
                            d = [tw, t_idf] + ([trds[-1]] if trds
                                               else ([prev_trd]
                                                     if prev_trd else []))
                        prev_ttr = S.step("tensor", lambda e, tb_=tb_, b=b:
                                          e.transpose(
                            PQ[:, 512 * b:512 * b + 128],
                            CM1[:, 128 * tb_:128 * tb_ + 128], TIDF[:, :]),
                            deps=d)
                    trds.append(S.step("vector", lambda e, r_=r_, pq3=pq3:
                                       e.tensor_reduce(
                        out=AMXS[:, 4 * r_:4 * r_ + 4], in_=pq3,
                        axis=AX.X, op=ALU.max, apply_absolute_value=True),
                        deps=[prev_ttr]))
                tmx = S.step("vector", lambda e: e.tensor_tensor(
                    out=AMX[:, :], in0=AMX[:, :], in1=AMXS[:, :], op=ALU.max),
                    deps=trds + [tmx])
                prev_trd = trds[-1]
            rowd = row_from_cols(SXO1[0:1, 0:T], AMX[:, 0:16], 16, [tmx])
            rmu = S.step("vector", lambda e: e.tensor_mul(
                out=SXO1[0:1, 0:T], in0=SXO1[0:1, 0:T], in1=SXC[0:1, 0:T]),
                deps=[rowd] + list(ln_toks))
            rcl = S.step("vector", lambda e: e.tensor_scalar(
                out=SXO1[0:1, 0:T], in0=SXO1[0:1, 0:T], scalar1=1e-12,
                scalar2=None, op0=ALU.max), deps=[rmu])
            rrec = recip_chain(CM1[0:1, 0:T], SXO1[0:1, 0:T], [rcl], scale=-1.0)
            r127 = S.step("vector", lambda e: e.tensor_scalar(
                out=CM1[0:1, 0:T], in0=CM1[0:1, 0:T], scalar1=127.0,
                scalar2=None, op0=ALU.mult), deps=[rrec])
            rqb = []
            for g in range(4):
                b1 = S.step("tensor", lambda e, g=g: e.matmul(
                    PQ[:, 512 * g:512 * g + 512], TONESRF[:, :],
                    CM1[0:1, 512 * g:512 * g + 512], start=True, stop=True),
                    deps=[r127, t_onesrf])
                rqb.append(S.step("vector", lambda e, g=g: e.tensor_copy(
                    out=AR2[:, 512 * g:512 * g + 512],
                    in_=PQ[:, 512 * g:512 * g + 512]), deps=[b1]))
            ta2 = S.step("vector", lambda e: e.tensor_mul(
                out=AR2[:, :], in0=AR2[:, :], in1=SXC[:, :]),
                deps=rqb + list(ln_toks))
            tb2 = S.step("vector", lambda e: e.scalar_tensor_tensor(
                out=CM0[:, :], in0=CM0[:, :], scalar=-1.0, in1=AR2[:, :],
                op0=ALU.mult, op1=ALU.mult), deps=[ta2])
            slr = S.step("vector", lambda e: e.tensor_scalar(
                out=SXC[0:1, 0:T], in0=SXO1[0:1, 0:T], scalar1=1.0 / 127.0,
                scalar2=None, op0=ALU.mult), deps=[ta2, rcl])
            tq = None
            tq_prev = {}
            pst = None
            for cc in range(NCH):
                t1b = SXO0 if cc % 2 == 0 else CM1
                if cc == 0:
                    dld = [tw]
                elif cc == 1:
                    dld = [slr]
                else:
                    dld = [tq_prev[cc % 2]]
                tld = sdma(t1b[:, :], src_hbm[cc * 128:(cc + 1) * 128, :],
                           deps=dld)
                t_a = S.step("gpsimd", lambda e, t1b=t1b: e.tensor_tensor(
                    out=t1b[:, :], in0=t1b[:, :], in1=AR2[:, :],
                    op=ALU.mult), deps=[tld, tb2, slr])
                t_b = S.step("gpsimd", lambda e, t1b=t1b: e.tensor_tensor(
                    out=t1b[:, :], in0=t1b[:, :], in1=CM0[:, :],
                    op=ALU.add), deps=[t_a])
                tq, pst = emit_cascade(t1b, T, ACT1[:, cc * T:(cc + 1) * T],
                                       [t_b],
                                       mbufs=[SXO1[:, 0:1024],
                                              SXO1[:, 1024:2048]],
                                       prev_st=pst)
                tq_prev[cc % 2] = tq
            slb = []
            for g in range(4):
                PSx = PS1 if g < 2 else PS2
                b1 = S.step("tensor", lambda e, g=g, PSx=PSx: e.matmul(
                    PSx[:, 512 * (g % 2):512 * (g % 2) + 512], TONESRF[:, :],
                    SXC[0:1, 512 * g:512 * g + 512], start=True, stop=True),
                    deps=[slr, t_onesrf])
                slb.append(S.step("vector", lambda e, g=g, PSx=PSx: e.tensor_copy(
                    out=AR2[:, 512 * g:512 * g + 512],
                    in_=PSx[:, 512 * (g % 2):512 * (g % 2) + 512]),
                    deps=[b1, tq]))
            evac_tok = [None, None]
            last_store = None
            st_toks = []
            for oc in range(NCH):
                mm = None
                for tg in range(4):
                    for cc in range(NCH):
                        d = []
                        if tg == 0 and cc == 0:
                            d = [tq] + (list(rhs_ready) if oc == 0 else [])
                            if oc == 1:
                                d.append(slb[-1])
                            if evac_tok[oc % 2] is not None:
                                d.append(evac_tok[oc % 2])
                        if oc % 2 == 0:
                            mm = S.step("tensor", lambda e, oc=oc, tg=tg, cc=cc:
                                        e.matmul(
                                PQ[:, 512 * tg:512 * tg + 512],
                                QRHS[:, cc * C + oc * 128:cc * C + oc * 128 + 128],
                                ACT1[:, cc * T + 512 * tg:cc * T + 512 * tg + 512],
                                start=(cc == 0), stop=(cc == NCH - 1)), deps=d)
                        else:
                            PSx = PS1 if tg < 2 else PS2
                            mm = S.step("tensor", lambda e, oc=oc, tg=tg, cc=cc,
                                        PSx=PSx: e.matmul(
                                PSx[:, 512 * (tg % 2):512 * (tg % 2) + 512],
                                QRHS[:, cc * C + oc * 128:cc * C + oc * 128 + 128],
                                ACT1[:, cc * T + 512 * tg:cc * T + 512 * tg + 512],
                                start=(cc == 0), stop=(cc == NCH - 1)), deps=d)
                xbuf = SXO0 if oc % 2 == 0 else SXO1
                t_old = sdma(xbuf[:, :], src_hbm[oc * 128:(oc + 1) * 128, :],
                             deps=([evac_tok[oc % 2]] if evac_tok[oc % 2]
                                   else [tq]))
                dls = [last_store] if last_store else []
                if oc % 2 == 0:
                    td1 = S.step("vector", lambda e, oc=oc: e.tensor_scalar(
                        out=CM0[:, :], in0=PQ[:, 0:2048],
                        scalar1=CORRC[:, oc:oc + 1], scalar2=SCCOL[:, oc:oc + 1],
                        op0=ALU.add, op1=ALU.mult), deps=[mm, t_old] + dls)
                else:
                    ta_ = S.step("vector", lambda e, oc=oc: e.tensor_scalar(
                        out=CM0[:, 0:1024], in0=PS1[:, :],
                        scalar1=CORRC[:, oc:oc + 1], scalar2=SCCOL[:, oc:oc + 1],
                        op0=ALU.add, op1=ALU.mult), deps=[mm, t_old] + dls)
                    td1 = S.step("vector", lambda e, oc=oc: e.tensor_scalar(
                        out=CM0[:, 1024:2048], in0=PS2[:, :],
                        scalar1=CORRC[:, oc:oc + 1], scalar2=SCCOL[:, oc:oc + 1],
                        op0=ALU.add, op1=ALU.mult), deps=[ta_])
                td2 = S.step("gpsimd", lambda e: e.tensor_tensor(
                    out=CM0[:, :], in0=CM0[:, :], in1=AR2[:, :],
                    op=ALU.mult), deps=[td1, slb[-1]])
                t_add = S.step("vector", lambda e, xbuf=xbuf: e.tensor_add(
                    out=SXC[:, :], in0=CM0[:, :], in1=xbuf[:, :]), deps=[td2])
                evac_tok[oc % 2] = t_add
                t_st = S.step("sync", lambda e, oc=oc: e.dma_start(
                    out=dst_hbm[oc * 128:(oc + 1) * 128, :], in_=SXC[:, :]),
                    dma=True, deps=[t_add])
                last_store = t_st
                st_toks.append(t_st)
            return st_toks

        wa_tok = [None]

        def load_wa(p8, deps):
            tks = []
            for cc in range(NCH):
                for part in range(3):
                    tks.append(gload(
                        WA[:, cc * 384 + part * 128:cc * 384 + part * 128 + 128],
                        waT[cc * 128:(cc + 1) * 128,
                            part * C + p8 * 128:part * C + p8 * 128 + 128],
                        deps=(deps if (cc == 0 and part == 0) else ())))
            wa_tok[0] = tks

        def prep_a_body(p8, gates):
            vt = VT0 if p8 % 2 == 0 else VT1
            wtok = wa_tok[0]
            thunks = []
            mus = {0: [], 1: []}
            tvt = []
            bank_free = {0: gates.get("pq23"), 1: gates.get("pq23")}
            cm_gate = list(gates.get("rope") or [])
            sb2_gate = list(gates.get("sb2") or [])
            vt_gate = list(gates.get("vt") or [])
            start_gate = list(gates.get("start") or [])
            mm_done = {}
            last_mm = [None]

            def th_mm(part, half, tgl):
                def f():
                    tg = 2 * half + tgl
                    mm = None
                    for cc in range(NCH):
                        d = []
                        if cc == 0:
                            d = [wtok]
                            if bank_free[tgl] is not None:
                                d.append(bank_free[tgl])
                            if part == 2 and half == 0 and tgl == 0:
                                d += start_gate
                        mm = S.step("tensor",
                                    lambda e, part=part, tg=tg, tgl=tgl,
                                    cc=cc: e.matmul(
                            PQ[:, (2 + tgl) * 512:(2 + tgl) * 512 + 512],
                            WA[:, cc * 384 + part * 128:
                               cc * 384 + part * 128 + 128],
                            ACT1[:, cc * T + 512 * tg:
                                 cc * T + 512 * tg + 512],
                            start=(cc == 0), stop=(cc == NCH - 1)),
                            deps=d)
                    mm_done[(part, half, tgl)] = mm
                    last_mm[0] = mm
                return f

            def th_sp(part, half):
                def f():
                    dst = SB0 if part == 0 else SB1
                    tx = S.step("scalar", lambda e, half=half: e.activation(
                        CM0[:, 1024 * half:1024 * half + 1024],
                        PQ[:, 1024:2048], AF.Exp),
                        deps=[mm_done[(part, half, 0)],
                              mm_done[(part, half, 1)]] + cm_gate)
                    t2 = S.step("scalar", lambda e, dst=dst, half=half:
                                e.activation(
                        dst[:, 1024 * half:1024 * half + 1024],
                        CM0[:, 1024 * half:1024 * half + 1024],
                        AF.Ln, bias=1.0), deps=[tx])
                    mus[part].append(t2)
                    bank_free[0] = tx
                    bank_free[1] = tx
                return f

            def th_ev(half):
                def f():
                    for tgl in (0, 1):
                        pc = S.step("vector", lambda e, half=half, tgl=tgl:
                                    e.tensor_copy(
                            out=SB2[:, 1024 * half + 512 * tgl:
                                    1024 * half + 512 * tgl + 512],
                            in_=PQ[:, (2 + tgl) * 512:(2 + tgl) * 512 + 512]),
                            deps=[mm_done[(2, half, tgl)]] +
                            (sb2_gate if (half == 0 and tgl == 0) else []))
                        bank_free[tgl] = pc
                        j0 = 8 * half + 4 * tgl
                        for j in range(j0, j0 + 4):
                            for hh2 in range(2):
                                d = ()
                                if hh2 == 0 and j == j0:
                                    d = [pc, t_vt0, t_vt1] + \
                                        (vt_gate if (half == 0 and tgl == 0)
                                         else [])
                                tvt.append(S.step("sync",
                                                  lambda e, j=j, hh2=hh2,
                                                  vt=vt: e.dma_start(
                                    out=vt[:, (16 * hh2 + j) * 80:
                                           (16 * hh2 + j) * 80 + 64],
                                    in_=SB2[64 * hh2:64 * hh2 + 64,
                                            128 * j:128 * j + 128],
                                    transpose=True), dma=True, deps=d))
                return f

            for part in (2, 0, 1):
                for half in (0, 1):
                    thunks.append(th_mm(part, half, 0))
                    thunks.append(th_mm(part, half, 1))
                    if part < 2:
                        thunks.append(th_sp(part, half))
                    else:
                        thunks.append(th_ev(half))
            if p8 + 1 < NP:
                def th_load():
                    load_wa(p8 + 1, [last_mm[0]])
                thunks.append(th_load)

            def fin(prev):
                toks_h = {}
                for hh in range(2):
                    r0, r1 = 64 * hh, 64 * hh + 64
                    QS = QS0 if hh == 0 else QS1
                    KS = KS0 if hh == 0 else KS1
                    rt1 = S.step("gpsimd", lambda e, QS=QS, r0=r0, r1=r1:
                                 e.tensor_tensor(
                        out=QS[0:64, :], in0=SB0[r0:r1, :],
                        in1=TCOS[r0:r1, :], op=ALU.mult),
                        deps=mus[0] + [t_cosP] + list(prev))
                    rt2 = S.step("gpsimd", lambda e, QS=QS, r0=r0, r1=r1:
                                 e.tensor_tensor(
                        out=QS[64:128, :], in0=SB0[r0:r1, :],
                        in1=TSIN[r0:r1, :], op=ALU.mult),
                        deps=mus[0] + [t_sinP])
                    tA = S.step("vector", lambda e, r0=r0, r1=r1:
                                e.tensor_mul(
                        out=CM0[r0:r1, :], in0=SB1[r0:r1, :],
                        in1=TCOS[r0:r1, :]),
                        deps=mus[1] + [t_cosP] + (list(prev) if hh == 0
                                                  else []))
                    tB = S.step("vector", lambda e, r0=r0, r1=r1:
                                e.tensor_mul(
                        out=CM1[r0:r1, :], in0=SB1[r0:r1, :],
                        in1=TSIN[r0:r1, :]), deps=mus[1] + [t_sinP])
                    tu = S.step("vector", lambda e, p8=p8, r0=r0, r1=r1:
                                e.tensor_scalar(
                        out=SXC[r0:r1, :], in0=CM1[r0:r1, :],
                        scalar1=TRIGS[r0:r1, p8:p8 + 1],
                        scalar2=None, op0=ALU.mult),
                        deps=[tB, getattr(S, "attn_sums_gate", None)] +
                        t_trig)
                    kr = S.step("vector", lambda e, KS=KS, r0=r0, r1=r1,
                                p8=p8: e.scalar_tensor_tensor(
                        out=KS[0:64, :], in0=CM0[r0:r1, :],
                        scalar=TRIGC[r0:r1, p8:p8 + 1],
                        in1=SXC[r0:r1, :],
                        op0=ALU.mult, op1=ALU.subtract), deps=[tA, tu])
                    tv2 = S.step("vector", lambda e, p8=p8, r0=r0, r1=r1:
                                 e.tensor_scalar(
                        out=SXC[r0:r1, :], in0=CM0[r0:r1, :],
                        scalar1=TRIGS[r0:r1, p8:p8 + 1],
                        scalar2=None, op0=ALU.mult), deps=[tA, kr])
                    ki = S.step("vector", lambda e, KS=KS, r0=r0, r1=r1,
                                p8=p8: e.scalar_tensor_tensor(
                        out=KS[64:128, :], in0=CM1[r0:r1, :],
                        scalar=TRIGC[r0:r1, p8:p8 + 1],
                        in1=SXC[r0:r1, :],
                        op0=ALU.mult, op1=ALU.add), deps=[tB, tv2])
                    toks_h[hh] = S.step("vector", lambda e: e.engine_nop(),
                                        deps=[rt1, rt2, kr, ki])
                tok = (toks_h[0], toks_h[1])
                return (tok, (QS0, QS1), (KS0, KS1), vt, tvt, [toks_h[1]],
                        bank_free[0], last_mm[0])
            return thunks, fin

        wqv_tok = [None, None]

        wq_lts = {}

        def build_wqv_load(p8, deps):
            qcol = p8 * 128
            if p8 < 4:
                wstart = 256 * p8
            else:
                wstart = 256 * (p8 - 4)
            lts = {}
            for i, cc in enumerate(range(4)):
                base = i * 896
                d0 = list(deps) if i == 0 else ()
                lt = []
                lt.append(gload(WSRC[:, base:base + 128],
                                wq[cc * 128:(cc + 1) * 128,
                                   qcol:qcol + 128], deps=d0))
                lt.append(gload(WSRC[:, base + 128:base + 384],
                                wq[cc * 128:(cc + 1) * 128,
                                   wstart:wstart + 256]))
                lt.append(gload(WSRC[:, base + 384:base + 512],
                                wq[cc * 128:(cc + 1) * 128,
                                   C + qcol:C + qcol + 128]))
                lt.append(gload(WSRC[:, base + 512:base + 768],
                                wq[cc * 128:(cc + 1) * 128,
                                   C + wstart:C + wstart + 256]))
                lt.append(gload(WSRC[:, base + 768:base + 896],
                                wq[cc * 128:(cc + 1) * 128,
                                   2 * C + qcol:2 * C + qcol + 128]))
                lts[cc] = lt
            wq_lts[p8] = lts

        def build_wqv(p8, deps):
            buf = WQV0 if p8 % 2 == 0 else WQV1
            qcol = p8 * 128
            if p8 < 4:
                wstart, wodd = 256 * p8, 1
            else:
                wstart, wodd = 256 * (p8 - 4), 0
            bts = []
            for (pi, row, dstc) in ((0, CROWC, 0), (0, CROWM, 1),
                                    (1, CROWC, 2), (1, CROWM, 3)):
                srow = row[0:1, qcol:qcol + 128]
                b1 = S.step("tensor", lambda e, srow=srow: e.matmul(
                    PQ[:, 1024:1152], TONESR[:, :], srow,
                    start=True, stop=True),
                    deps=[t_crc, t_crm, t_onesr] + list(deps) +
                    ([bts[-1]] if bts else []))
                bts.append(S.step("vector", lambda e, dstc=dstc: e.tensor_copy(
                    out=CSG[:, dstc * 128:(dstc + 1) * 128],
                    in_=PQ[:, 1024:1152]),
                    deps=[b1]))
            btok = bts[-1]
            toks = []
            for half in range(2):
                ccs = list(range(half * 4, half * 4 + 4))
                if half == 0:
                    lts = wq_lts.pop(p8)
                else:
                    lts = {}
                    for i, cc in enumerate(ccs):
                        base = i * 896
                        d0 = (list(deps) + toks[-1:]) if i == 0 else ()
                        lt = []
                        lt.append(gload(WSRC[:, base:base + 128],
                                        wq[cc * 128:(cc + 1) * 128,
                                           qcol:qcol + 128], deps=d0))
                        lt.append(gload(WSRC[:, base + 128:base + 384],
                                        wq[cc * 128:(cc + 1) * 128,
                                           wstart:wstart + 256]))
                        lt.append(gload(WSRC[:, base + 384:base + 512],
                                        wq[cc * 128:(cc + 1) * 128,
                                           C + qcol:C + qcol + 128]))
                        lt.append(gload(WSRC[:, base + 512:base + 768],
                                        wq[cc * 128:(cc + 1) * 128,
                                           C + wstart:C + wstart + 256]))
                        lt.append(gload(WSRC[:, base + 768:base + 896],
                                        wq[cc * 128:(cc + 1) * 128,
                                           2 * C + qcol:2 * C + qcol + 128]))
                        lts[cc] = lt
                for i, cc in enumerate(ccs):
                    base = i * 896
                    for part in range(2):
                        mainv = WSRC[:, base + part * 384:base + part * 384 + 128]
                        ws = base + part * 384 + 128 + wodd
                        wv = WSRC[:, ws:ws + 255:2]
                        cosb = CSG[:, (2 * part) * 128:(2 * part) * 128 + 128]
                        msb = CSG[:, (2 * part + 1) * 128:(2 * part + 1) * 128 + 128]
                        dst = buf[:, cc * 384 + part * 128:cc * 384 + part * 128 + 128]
                        a1 = S.step("vector", lambda e, mainv=mainv, cosb=cosb,
                                    dst=dst, cc=cc: e.scalar_tensor_tensor(
                            out=dst, in0=mainv, scalar=TGAM[:, cc:cc + 1], in1=cosb,
                            op0=ALU.mult, op1=ALU.mult),
                            deps=lts[cc] + [btok, t_gam] +
                            ([toks[-1]] if toks else []))
                        a2 = S.step("vector", lambda e, wv=wv, msb=msb, cc=cc:
                                    e.scalar_tensor_tensor(
                            out=CSG[:, 512:640], in0=wv,
                            scalar=TGAM[:, cc:cc + 1],
                            in1=msb, op0=ALU.mult, op1=ALU.mult), deps=[a1])
                        a3 = S.step("vector", lambda e, dst=dst: e.tensor_add(
                            out=dst, in0=dst, in1=CSG[:, 512:640]), deps=[a2])
                        toks.append(a3)
                    dstv = buf[:, cc * 384 + 256:cc * 384 + 384]
                    toks.append(S.step("vector", lambda e, base=base, dstv=dstv, cc=cc:
                                       e.tensor_scalar(
                        out=dstv, in0=WSRC[:, base + 768:base + 896],
                        scalar1=TGAM[:, cc:cc + 1], scalar2=None, op0=ALU.mult),
                        deps=lts[cc]))
            wqv_tok[p8 % 2] = toks[-1]
            return btok

        def emit_bg(deps):
            t2 = S.step("scalar", lambda e: e.activation(CM0[:, 0:NCH], TGAM[:, :],
                                                         AF.Abs),
                        deps=[t_gam, t_bet] + list(deps))
            t3 = S.step("scalar", lambda e: e.activation(CM0[:, 0:NCH], CM0[:, 0:NCH],
                                                         AF.Ln), deps=[t2])
            t4 = S.step("scalar", lambda e: e.activation(CM0[:, 0:NCH], CM0[:, 0:NCH],
                                                         AF.Exp, scale=-1.0), deps=[t3])
            t5 = S.step("scalar", lambda e: e.activation(CM0[:, NCH:2 * NCH],
                                                         TGAM[:, :], AF.Sign),
                        deps=[t4])
            t6 = S.step("vector", lambda e: e.tensor_mul(
                out=CM0[:, 0:NCH], in0=CM0[:, 0:NCH], in1=CM0[:, NCH:2 * NCH]),
                deps=[t5])
            t7 = S.step("vector", lambda e: e.tensor_mul(
                out=CM0[:, 0:NCH], in0=CM0[:, 0:NCH], in1=TBET[:, :]), deps=[t6])
            return S.step("vector", lambda e: e.tensor_copy(
                out=BGCOL[:, :], in_=CM0[:, 0:NCH]), deps=[t7])

        def make_prep_b_body(bg_tok):
            def prep_b_body(p8, gates):
                vt = VT0 if p8 % 2 == 0 else VT1
                buf = WQV0 if p8 % 2 == 0 else WQV1
                thunks = []
                tvt = []
                bank_free = {0: gates.get("pq23"), 1: gates.get("pq23")}
                start_gate = list(gates.get("start") or [])
                sb2_gate = list(gates.get("sb2") or [])
                vt_gate = list(gates.get("vt") or [])
                free_gate = list(gates.get("rope") or [])
                mm_done = {}
                ev_done = {}
                last_mm = [None]
                sq_tok = {}

                if p8 == 0:
                    build_wqv_load(0, start_gate)

                def th_build():
                    btok = build_wqv(
                        p8, start_gate + free_gate +
                        ([gates["wq_free"]] if gates.get("wq_free") else []) +
                        ([gates["pq23"]] if gates.get("pq23") else []))
                    bank_free[0] = btok
                    bank_free[1] = btok
                thunks.append(th_build)

                def th_mm(part, half, tgl):
                    def f():
                        tg = 2 * half + tgl
                        mm = None
                        for cc in range(NCH):
                            d = []
                            if cc == 0:
                                d = [wqv_tok[p8 % 2]]
                                if bank_free[tgl] is not None:
                                    d.append(bank_free[tgl])
                                if part == 2 and half == 0 and tgl == 0:
                                    d += start_gate
                            mm = S.step("tensor",
                                        lambda e, part=part, tg=tg, tgl=tgl,
                                        cc=cc, buf=buf: e.matmul(
                                PQ[:, (2 + tgl) * 512:(2 + tgl) * 512 + 512],
                                buf[:, cc * 384 + part * 128:
                                    cc * 384 + part * 128 + 128],
                                ACT1[:, cc * T + 512 * tg:
                                     cc * T + 512 * tg + 512],
                                start=(cc == 0), stop=(cc == NCH - 1)),
                                deps=d)
                        mm_done[(part, half, tgl)] = mm
                        last_mm[0] = mm
                    return f

                def th_evc(part, half):
                    def f():
                        dst = (SB0, SB1, SB2)[part]
                        for tgl in (0, 1):
                            pc = S.step("vector", lambda e, dst=dst, half=half,
                                        tgl=tgl: e.tensor_copy(
                                out=dst[:, 1024 * half + 512 * tgl:
                                        1024 * half + 512 * tgl + 512],
                                in_=PQ[:, (2 + tgl) * 512:
                                       (2 + tgl) * 512 + 512]),
                                deps=[mm_done[(part, half, tgl)]] +
                                (sb2_gate if (part == 2 and half == 0 and
                                              tgl == 0) else []) +
                                (free_gate if (part != 2 and half == 0 and
                                               tgl == 0) else []))
                            bank_free[tgl] = pc
                            ev_done[(part, half, tgl)] = pc
                    return f

                def th_bias(part):
                    def f():
                        dst = (SB0, SB1, SB2)[part]
                        bm = None
                        for cc in range(NCH):
                            d = []
                            if cc == 0:
                                d = [bg_tok, wqv_tok[p8 % 2],
                                     bank_free[0], bank_free[1]]
                            bm = S.step("tensor", lambda e, cc=cc, part=part,
                                        buf=buf: e.matmul(
                                PQ[0:1, 1024:1152], BGCOL[:, cc:cc + 1],
                                buf[:, cc * 384 + part * 128:
                                    cc * 384 + part * 128 + 128],
                                start=(cc == 0), stop=(cc == NCH - 1)),
                                deps=d)
                        bev = S.step("vector", lambda e: e.tensor_copy(
                            out=BRH[0:1, 0:128], in_=PQ[0:1, 1024:1152]),
                            deps=[bm])
                        bt2 = S.step("tensor", lambda e: e.matmul(
                            PQ[:, 1536:1537], BRH[0:1, 0:128], TONESTF[:, :],
                            start=True, stop=True), deps=[bev, t_onestf])
                        bcol = S.step("vector", lambda e, part=part, p8=p8:
                                      e.tensor_copy(
                            out=BIASC[:, part * NCH + p8:part * NCH + p8 + 1],
                            in_=PQ[:, 1536:1537]), deps=[bt2])
                        bank_free[0] = bev
                        bank_free[1] = bcol
                        badds = []
                        for half in (0, 1):
                            badds.append(S.step("vector",
                                               lambda e, dst=dst, part=part,
                                               half=half, p8=p8:
                                               e.tensor_scalar(
                                out=dst[:, 1024 * half:1024 * half + 1024],
                                in0=dst[:, 1024 * half:1024 * half + 1024],
                                scalar1=BIASC[:, part * NCH + p8:
                                              part * NCH + p8 + 1],
                                scalar2=None, op0=ALU.add), deps=[bcol]))
                        if part == 2:
                            for j in range(16):
                                for hh2 in range(2):
                                    d = ()
                                    if hh2 == 0 and j % 8 == 0:
                                        d = [badds[j // 8], t_vt0, t_vt1] + \
                                            (vt_gate if j == 0 else [])
                                    tvt.append(S.step("sync",
                                                      lambda e, j=j, hh2=hh2,
                                                      vt=vt: e.dma_start(
                                        out=vt[:, (16 * hh2 + j) * 80:
                                               (16 * hh2 + j) * 80 + 64],
                                        in_=SB2[64 * hh2:64 * hh2 + 64,
                                                128 * j:128 * j + 128],
                                        transpose=True), dma=True, deps=d))
                        elif part == 0:
                            sq_tok[0] = S.step("vector", lambda e:
                                               e.tensor_mul(
                                out=CSG[:, :], in0=SB0[:, :], in1=SB0[:, :]),
                                deps=badds + free_gate)
                        else:
                            sq_tok[1] = S.step("vector",
                                               lambda e: e.engine_nop(),
                                               deps=badds + free_gate)
                    return f

                for part in (2, 0, 1):
                    for half in (0, 1):
                        thunks.append(th_mm(part, half, 0))
                        thunks.append(th_mm(part, half, 1))
                        thunks.append(th_evc(part, half))
                    thunks.append(th_bias(part))
                if p8 + 1 < NP:
                    def th_preload():
                        build_wqv_load(p8 + 1, [wqv_tok[p8 % 2]])
                    thunks.append(th_preload)

                def fin(prev):
                    nt = []
                    for g in range(4):
                        d = [sq_tok[0], sq_tok[1], t_sel2t]
                        if g == 0:
                            d += list(prev)
                        nt.append(S.step("tensor", lambda e, g=g: e.matmul(
                            PQ[0:2, 512 * g:512 * g + 512],
                            TSEL2T[:, :], CSG[:, 512 * g:512 * g + 512],
                            start=True, stop=True,
                            tile_position=(0, 0)), deps=d))
                    sqk = S.step("vector", lambda e: e.tensor_mul(
                        out=CSG[:, :], in0=SB1[:, :], in1=SB1[:, :]),
                        deps=nt + [sq_tok[1]])
                    for g in range(4):
                        nt.append(S.step("tensor", lambda e, g=g: e.matmul(
                            PQ[32:34, 512 * g:512 * g + 512],
                            TSEL2T[:, :], CSG[:, 512 * g:512 * g + 512],
                            start=True, stop=True,
                            tile_position=(0, 32)), deps=[sqk, t_sel2t]))
                    tevn = S.step("scalar", lambda e: e.copy(
                        N2[:, :], PQ[0:34, 0:2048]), deps=nt)
                    tmx = S.step("vector", lambda e: e.tensor_scalar(
                        out=N2[:, :], in0=N2[:, :], scalar1=1e-24,
                        scalar2=None, op0=ALU.max), deps=[tevn])
                    tn1 = S.step("scalar", lambda e: e.activation(
                        N2[:, :], N2[:, :], AF.Ln), deps=[tmx])
                    ntoks = []
                    for part in range(2):
                        src = SB0 if part == 0 else SB1
                        dst = QS0 if part == 0 else KS0
                        tl2 = S.step("scalar", lambda e, part=part:
                                     e.activation(
                            CSG[0:2, :], N2[32 * part:32 * part + 2, :],
                            AF.Exp, scale=-0.5),
                            deps=[tn1] + ([ntoks[-1]] if ntoks else []))
                        for g in range(4):
                            bb = S.step("tensor", lambda e, g=g: e.matmul(
                                PQ[:, 512 * (g % 2):512 * (g % 2) + 512],
                                TSEL2[:, :], CSG[0:2, 512 * g:512 * g + 512],
                                start=True, stop=True),
                                deps=[tl2, t_sel2] +
                                ([ntoks[-1]] if ntoks else [tevn]))
                            dst2 = QS1 if part == 0 else KS1
                            ntoks.append(S.step("vector", lambda e, g=g,
                                                part=part, src=src, dst=dst:
                                                e.tensor_mul(
                                out=dst[0:64, 512 * g:512 * g + 512],
                                in0=src[0:64, 512 * g:512 * g + 512],
                                in1=PQ[0:64, 512 * (g % 2):512 * (g % 2) + 512]),
                                deps=[bb]))
                            ntoks.append(S.step("vector", lambda e, g=g,
                                                part=part, src=src, dst2=dst2:
                                                e.tensor_mul(
                                out=dst2[64:128, 512 * g:512 * g + 512],
                                in0=src[64:128, 512 * g:512 * g + 512],
                                in1=PQ[64:128,
                                       512 * (g % 2):512 * (g % 2) + 512]),
                                deps=[bb]))
                    tok = S.step("vector", lambda e: e.engine_nop(),
                                 deps=ntoks)
                    return (tok, (QS0, QS1), (KS0, KS1), vt, tvt, [tok],
                            tevn, last_mm[0])
                return thunks, fin
            return prep_b_body

        setup = [t_qkcol, t_vt0, t_vt1, t_msk, t_id] + t_trig
        sc_toks, am_last = emit_rhs_prep_scales([])
        rhs_aux = make_rhs_q_thunks(sc_toks, am_last)

        ping = [Xd0, Xd1]
        cur_src = xT

        ln_toks = emit_ln(cur_src, setup + sc_toks)
        load_wa(0, [])
        ms0 = S.step("vector", lambda e: e.memset(PS1[:, :], 0.0),
                     deps=ln_toks + sc_toks)
        ms1 = S.step("vector", lambda e: e.memset(PS2[:, :], 0.0), deps=[ms0])
        evacs = emit_attention("a", None, None, 0.125, None,
                               ln_toks + [ms1] + setup,
                               body_maker=prep_a_body, aux=rhs_aux)
        rhs_ready = emit_rhs_colsums([S.tok("scalar"), S.tok("vector")])
        ntoks = emit_norm_oall(evacs)
        stores = emit_proj(wpT, cur_src, ping[0],
                           ntoks + [e[0] for e in evacs] + rhs_ready)
        cur_src = ping[0]

        if n_sub >= 2:
            ln_toks = emit_ln(cur_src, stores, apply=False)
            stores = emit_ffn(cur_src, ping[1], ln_toks, rhs_ready)
            cur_src = ping[1]
        if n_sub >= 3:
            bg_tok = emit_bg(stores)
            ln_toks = emit_ln(cur_src, stores + [bg_tok], gamma_mode=True)
            ms0 = S.step("vector", lambda e: e.memset(PS1[:, :], 0.0), deps=ln_toks)
            ms1 = S.step("vector", lambda e: e.memset(PS2[:, :], 0.0), deps=[ms0])
            msz = [S.step("vector", lambda e, tile=tile, lohi=lohi: e.memset(
                tile[lohi[0]:lohi[1], :], 0.0), deps=list(ln_toks))
                for (tile, lohi) in ((QS0, (64, 128)), (QS1, (0, 64)),
                                     (KS0, (64, 128)), (KS1, (0, 64)))]
            evacs = emit_attention("b", None, None,
                                   None, QKCOL[:, 0:1], ln_toks + [ms1, bg_tok] + msz,
                                   body_maker=make_prep_b_body(bg_tok))
            ntoks = emit_norm_oall(evacs)
            stores = emit_proj(wo, cur_src, ping[0], ntoks + [e[0] for e in evacs])
            cur_src = ping[0]
        if n_sub >= 4:
            ln_toks = emit_ln(cur_src, stores, apply=False)
            stores = emit_ffn(cur_src, yout, ln_toks, rhs_ready)
        else:
            for cc in range(NCH):
                tl = sdma(SXC[:, :], cur_src[cc * 128:(cc + 1) * 128, :], deps=stores)
                stores = [S.step("sync", lambda e, cc=cc: e.dma_start(
                    out=yout[cc * 128:(cc + 1) * 128, :], in_=SXC[:, :]),
                    dma=True, deps=[tl])]

        S.emit(blk, sems)
    lp.__exit__(None, None, None)
    return nc


_nc_cache = {}


def kernel(L0_attn_a_x, L0_attn_a_w_attn, L0_attn_a_w_proj, L0_attn_a_delta,
           L0_attn_b_ln_gamma, L0_attn_b_ln_beta, L0_attn_b_w_qkv, L0_attn_b_w_out,
           L0_attn_b_qkscale, L0_ffn_rhs):
    x = np.asarray(L0_attn_a_x, np.float32)
    tb = host_tables()
    cc = np.ascontiguousarray

    delta = np.asarray(L0_attn_a_delta, np.float32)
    delta_pc = np.zeros((128, NP), np.float32)
    for p8 in range(NP):
        delta_pc[0:64, p8] = delta[2 * p8]
        delta_pc[64:128, p8] = delta[2 * p8 + 1]

    gamma = np.asarray(L0_attn_b_ln_gamma, np.float32)
    beta = np.asarray(L0_attn_b_ln_beta, np.float32)

    base = {
        "waT": cc(np.asarray(L0_attn_a_w_attn, np.float32).T),
        "wpT": cc(np.asarray(L0_attn_a_w_proj, np.float32).T),
        "wq": cc(np.asarray(L0_attn_b_w_qkv, np.float32)),
        "wo": cc(np.asarray(L0_attn_b_w_out, np.float32)),
        "rhs": cc(np.asarray(L0_ffn_rhs, np.float32)),
        "rhsT": cc(np.asarray(L0_ffn_rhs, np.float32).T),
        "delta_pc": delta_pc,
        "gamma_fm": cc(gamma.reshape(NCH, 128).T),
        "beta_fm": cc(beta.reshape(NCH, 128).T),
        "qks": np.asarray(L0_attn_b_qkscale, np.float32).reshape(1, 1),
    }
    base.update(tb)

    key = (N_SUBLAYERS, DBG)
    if key not in _nc_cache:
        _nc_cache[key] = build_nc(N_SUBLAYERS)
    nc = _nc_cache[key]

    in_maps = []
    for b in range(8):
        m = dict(base)
        m["xT"] = cc(x[b].T)
        in_maps.append(m)
    res = run_bass_kernel_spmd(nc, in_maps, core_ids=list(range(8)))
    out = np.stack([np.asarray(r["y"], np.float32).T for r in res.results], axis=0)
    return out


# version 24
# speedup vs baseline: 1.0202x; 1.0202x over previous
import math
import numpy as np
import concourse.bass as bass
import concourse.mybir as mybir
from concourse.bass_utils import run_bass_kernel_spmd

F32 = mybir.dt.float32
BF16 = mybir.dt.bfloat16
AF = mybir.ActivationFunctionType
ALU = mybir.AluOpType
AX = mybir.AxisListType

T = 2048
C = 1024
H = 16
D = 64
NCH = 8
NP = 8
NEG = -1.0e30
LN_EPS = 1e-5
CAS_T = [128.0, 64.0, 32.0, 16.0, 8.0, 4.0, 2.0, 1.0]

N_SUBLAYERS = 4
DBG = False

ENGINES = ("sync", "tensor", "vector", "scalar", "gpsimd")


_trig_cache = {}


def _trig_tables():
    if "v" in _trig_cache:
        return _trig_cache["v"]
    import jax
    import jax.numpy as jnp
    with jax.default_device(jax.devices("cpu")[0]):
        inv_freq = 1.0 / 10000.0 ** (jnp.arange(D, dtype=jnp.float32) / D)
        t_idx = jnp.arange(T, dtype=jnp.float32)
        freqs = t_idx[:, None] * inv_freq[None, :]
        cosP = np.asarray(jnp.cos(freqs)).T.copy()
        sinP = np.asarray(jnp.sin(freqs)).T.copy()
        inv_b = 10000.0 ** (-(jnp.arange(0, C, 2, dtype=jnp.float32) / C))
        emb = jnp.repeat(inv_b, 2)
        sinr = np.asarray(jnp.sin(T * emb))
        cosr = np.asarray(jnp.cos(T * emb))
    _trig_cache["v"] = (cosP, sinP, cosr, sinr)
    return _trig_cache["v"]


def host_tables():
    tb = {}
    cosP, sinP, cosr, sinr = _trig_tables()
    tb["cosP"] = np.vstack([cosP] * 2).astype(np.float32)
    tb["sinP"] = np.vstack([sinP] * 2).astype(np.float32)

    msin = np.where(np.arange(C) < C // 2, -sinr, sinr)
    tb["cosrow"] = cosr.astype(np.float32).reshape(1, C)
    tb["msinrow"] = msin.astype(np.float32).reshape(1, C)

    m = np.zeros((128, 128), np.float32)
    for p in range(128):
        m[p, :p] = NEG
    tb["maskT"] = m
    tb["ident"] = np.eye(128, dtype=np.float32)
    s2 = np.zeros((2, 128), np.float32)
    s2[0, 0:64] = 1.0
    s2[1, 64:128] = 1.0
    tb["sel2"] = s2
    s2t = np.zeros((128, 2), np.float32)
    s2t[0:64, 0] = 1.0
    s2t[64:128, 1] = 1.0
    tb["sel2T"] = s2t
    selh = np.zeros((H, NP * 128), np.float32)
    for p8 in range(NP):
        selh[2 * p8, p8 * 128:p8 * 128 + 64] = 1.0
        selh[2 * p8 + 1, p8 * 128 + 64:p8 * 128 + 128] = 1.0
    tb["selh"] = selh
    cv = np.zeros((128, 9), np.float32)
    for i, v in enumerate([0.5, 1.0, 1e-5, 0.125, -0.5, -1.0, 0.0, 2.0,
                           1e-24]):
        cv[:, i] = v
    tb["constv"] = cv
    tb["onesc"] = np.ones((128, 1), np.float32)
    tb["onesr"] = np.ones((1, 128), np.float32)
    tb["onest"] = np.ones((1, 1), np.float32)
    return tb


DMA_RING = 16


class Sched:

    def __init__(self):
        self.steps = {e: [] for e in ENGINES}
        self.count = {e: 0 for e in ENGINES}
        self.dma_idx = {e: 0 for e in ENGINES}
        self.sem_names = list(ENGINES)
        for e in ("sync", "gpsimd", "scalar"):
            for i in range(DMA_RING):
                self.sem_names.append(f"{e}_d{i}")

    def step(self, eng, fn, deps=(), dma=False):
        dd = []
        for t in _flat(deps):
            if t is None:
                continue
            de, dv, ddma = t
            dd.append((de, dv))
        if not dma and eng != "tensor" and self.count[eng] > 0:
            dd.append((eng, self.count[eng]))
        if dma:
            i = self.dma_idx[eng]
            self.dma_idx[eng] += 1
            slot = f"{eng}_d{i % DMA_RING}"
            gen = i // DMA_RING
            ring_wait = (slot, 16 * gen)
            self.steps[eng].append(("dma", fn, dd, slot, ring_wait))
            return (slot, 16 * (gen + 1), True)
        self.steps[eng].append(("op", fn, dd, None, None))
        self.count[eng] += 1
        return (eng, self.count[eng], False)

    def tok(self, eng):
        return (eng, self.count[eng], False)

    def emit(self, blk, sems):
        final = dict(self.count)
        dma_final = {}
        for e in ENGINES:
            n = self.dma_idx.get(e, 0)
            for i in range(min(n, DMA_RING)):
                cnt = (n - i + DMA_RING - 1) // DMA_RING
                dma_final[f"{e}_d{i}"] = 16 * cnt
        barrier = getattr(self, "setup_barrier", [])

        def body_for(name):
            def body(eng):
                hw = {s: 0 for s in self.sem_names}
                for (bsem, bval, _bd) in barrier:
                    if bsem == name or bsem.startswith(name + "_d"):
                        continue
                    if bval > hw[bsem]:
                        eng.wait_ge(sems[bsem], bval)
                        hw[bsem] = bval
                for (kind, fn, deps, slot, ring_wait) in self.steps[name]:
                    for (dsem, dval) in deps:
                        if name == "tensor" and dsem == "tensor":
                            continue
                        if dval > hw[dsem]:
                            eng.wait_ge(sems[dsem], dval)
                            hw[dsem] = dval
                    if kind == "dma":
                        (rsem, rval) = ring_wait
                        if rval > hw[rsem]:
                            eng.wait_ge(sems[rsem], rval)
                            hw[rsem] = rval
                        fn(eng).then_inc(sems[slot], 16)
                    else:
                        fn(eng).then_inc(sems[name], 1)
                for e2 in ENGINES:
                    if e2 != name and final[e2] > 0:
                        eng.wait_ge(sems[e2], final[e2])
                for sname, v in dma_final.items():
                    if not sname.startswith(name) and v > hw.get(sname, 0):
                        eng.wait_ge(sems[sname], v)
            return body

        blk.sync(body_for("sync"))
        blk.tensor(body_for("tensor"))
        blk.vector(body_for("vector"))
        blk.scalar(body_for("scalar"))
        blk.gpsimd(body_for("gpsimd"))


def _flat(x):
    out = []
    for t in x:
        if isinstance(t, list):
            out += _flat(t)
        else:
            out.append(t)
    return out


def build_nc(n_sub=None):
    if n_sub is None:
        n_sub = N_SUBLAYERS
    nc = bass.Bass("TRN2", target_bir_lowering=False)
    lp = nc.allow_low_precision("bf16 operands sanctioned by problem statement")
    lp.__enter__()

    def din(name, shape):
        return nc.dram_tensor(name, list(shape), F32, kind="ExternalInput").ap()

    xT = din("xT", (C, T))
    waT = din("waT", (C, 3 * C))
    wpT = din("wpT", (C, C))
    wq = din("wq", (C, 3 * C))
    wo = din("wo", (C, C))
    rhs = din("rhs", (C, C))
    rhsT = din("rhsT", (C, C))
    delta_pc = din("delta_pc", (128, NP))
    gamma_fm = din("gamma_fm", (128, NCH))
    beta_fm = din("beta_fm", (128, NCH))
    qks = din("qks", (1, 1))
    cosPd = din("cosP", (128, T))
    sinPd = din("sinP", (128, T))
    cosrow = din("cosrow", (1, C))
    msinrow = din("msinrow", (1, C))
    maskTd = din("maskT", (128, 128))
    identd = din("ident", (128, 128))
    sel2d = din("sel2", (2, 128))
    sel2Td = din("sel2T", (128, 2))
    selhd = din("selh", (H, NP * 128))
    onescd = din("onesc", (128, 1))
    onesrd = din("onesr", (1, 128))
    onestd = din("onest", (1, 1))
    constvd = din("constv", (128, 9))

    yout = nc.dram_tensor("y", [C, T], F32, kind="ExternalOutput").ap()
    dbg = {}
    if DBG:
        for nm, shp, dty in [("dbgqs0", [128, T], BF16), ("dbgqs1", [128, T], BF16),
                             ("dbgks0", [128, T], BF16), ("dbgks1", [128, T], BF16),
                             ("dbgvt0", [128, 32 * 80], BF16),
                             ("dbgsb0", [128, T], BF16), ("dbgsb1", [128, T], BF16),
                             ("dbgsb2", [128, T], BF16),
                             ("dbgatt0", [128, 1024], BF16),
                             ("dbgatt1", [128, 1024], BF16),
                             ("dbgoall", [128, T], BF16),
                             ("dbgsums", [16, T], F32)]:
            dbg[nm] = nc.dram_tensor(nm, shp, dty, kind="ExternalOutput").ap()
    Xd0 = nc.dram_tensor("Xd0", [C, T], F32, kind="Internal").ap()
    Xd1 = nc.dram_tensor("Xd1", [C, T], F32, kind="Internal").ap()

    from contextlib import ExitStack
    st = ExitStack()
    e_ = st.enter_context
    with st:
        ACT1 = e_(nc.sbuf_tensor("ACT1", [128, NCH * T], BF16))
        OALL = e_(nc.sbuf_tensor("OALL", [128, NCH * T], BF16))
        CM0 = e_(nc.sbuf_tensor("CM0", [128, T], F32))
        CM1 = e_(nc.sbuf_tensor("CM1", [128, T], F32))
        SB0 = e_(nc.sbuf_tensor("SB0", [128, T], BF16))
        SB1 = e_(nc.sbuf_tensor("SB1", [128, T], BF16))
        SB2 = e_(nc.sbuf_tensor("SB2", [128, T], BF16))
        QS0 = e_(nc.sbuf_tensor("QS0", [128, T], BF16))
        QS1 = e_(nc.sbuf_tensor("QS1", [128, T], BF16))
        KS0 = e_(nc.sbuf_tensor("KS0", [128, T], BF16))
        KS1 = e_(nc.sbuf_tensor("KS1", [128, T], BF16))
        SB3 = QS1
        SB4 = KS1
        VT0 = e_(nc.sbuf_tensor("VT0", [128, 32 * 80], BF16))
        VT1 = e_(nc.sbuf_tensor("VT1", [128, 32 * 80], BF16))
        ATT0 = e_(nc.sbuf_tensor("ATT0", [128, 1024], BF16))
        ATT1 = e_(nc.sbuf_tensor("ATT1", [128, 1024], BF16))
        WA = e_(nc.sbuf_tensor("WA", [128, NCH * 384], BF16))
        WPB0 = WA[:, 0:1024]
        WPB1 = WA[:, 1024:2048]
        QRHS = e_(nc.sbuf_tensor("QRHS", [128, NCH * C], BF16))
        WQV0 = e_(nc.sbuf_tensor("WQV0", [128, NCH * 384], BF16))
        WQV1 = e_(nc.sbuf_tensor("WQV1", [128, NCH * 384], BF16))
        AR1 = e_(nc.sbuf_tensor("AR1", [128, 4096], BF16))
        TCOS = AR1[:, 0:2048]
        TSIN = AR1[:, 2048:4096]
        WSRC = AR1[:, 0:3584]
        SXO0 = e_(nc.sbuf_tensor("SXO0", [128, T], F32))
        SXO1 = e_(nc.sbuf_tensor("SXO1", [128, T], F32))
        SXC = e_(nc.sbuf_tensor("SXC", [128, T], F32))
        AR2 = e_(nc.sbuf_tensor("AR2", [128, T], F32))
        N2 = SXO0[0:34, 0:T]
        SUMS = AR2[0:H, 0:T]
        RSTG = SXO0[:, 0:C]
        CSG = e_(nc.sbuf_tensor("CSG", [128, T], BF16))
        TMSK = e_(nc.sbuf_tensor("TMSK", [128, 128], BF16))
        TID = e_(nc.sbuf_tensor("TID", [128, 128], BF16))
        TIDF = e_(nc.sbuf_tensor("TIDF", [128, 128], F32))
        TR16 = e_(nc.sbuf_tensor("TR16", [16, 128], F32))
        TSEL2 = e_(nc.sbuf_tensor("TSEL2", [2, 128], BF16))
        TSEL2T = e_(nc.sbuf_tensor("TSEL2T", [128, 2], BF16))
        TONESC = e_(nc.sbuf_tensor("TONESC", [128, 1], BF16))
        TONESR = e_(nc.sbuf_tensor("TONESR", [1, 128], BF16))
        TONESRF = e_(nc.sbuf_tensor("TONESRF", [1, 128], F32))
        TONESTF = e_(nc.sbuf_tensor("TONESTF", [1, 1], F32))
        TGAM = e_(nc.sbuf_tensor("TGAM", [128, NCH], F32))
        TBET = e_(nc.sbuf_tensor("TBET", [128, NCH], F32))
        TDELPC = e_(nc.sbuf_tensor("TDELPC", [128, NP], F32))
        TRIGC = e_(nc.sbuf_tensor("TRIGC", [128, NP], F32))
        TRIGS = e_(nc.sbuf_tensor("TRIGS", [128, NP], F32))
        TQKS = e_(nc.sbuf_tensor("TQKS", [1, 1], F32))
        TCONST = e_(nc.sbuf_tensor("TCONST", [128, 9], F32))
        QKCOL = e_(nc.sbuf_tensor("QKCOL", [128, 1], F32))
        SCCOL = e_(nc.sbuf_tensor("SCCOL", [128, NCH], F32))
        CORRC = e_(nc.sbuf_tensor("CORRC", [128, NCH], F32))
        BIASC = e_(nc.sbuf_tensor("BIASC", [128, 3 * NCH], F32))
        BGCOL = e_(nc.sbuf_tensor("BGCOL", [128, NCH], BF16))
        AMX = e_(nc.sbuf_tensor("AMX", [128, 16], F32))
        AMXS = e_(nc.sbuf_tensor("AMXS", [128, 16], F32))
        CROWC = e_(nc.sbuf_tensor("CROWC", [1, C], BF16))
        CROWM = e_(nc.sbuf_tensor("CROWM", [1, C], BF16))
        SROW = SXC[0:1, 0:T]
        SROW2 = CM1[0:1, 0:T]
        BRH = CM0[0:1, 1024:2048]
        TSELHF = CM1[0:H, 0:1024]

        PQ = e_(nc.psum_tensor("PQ", [128, 2048], F32))
        PS1 = e_(nc.psum_tensor("PS1", [128, 1024], F32))
        PS2 = e_(nc.psum_tensor("PS2", [128, 1024], F32))

        S = Sched()
        sems = {n: e_(nc.semaphore("sem_" + n)) for n in S.sem_names}
        blk = e_(nc.Block())

        def gload(dst, src, deps=()):
            return S.step("gpsimd", lambda e, d=dst, s=src: e.dma_start(out=d, in_=s),
                          deps=deps, dma=True)

        def sdma(dst, src, deps=()):
            return S.step("sync", lambda e, d=dst, s=src: e.dma_start(out=d, in_=s),
                          deps=deps, dma=True)

        t_cosP = gload(TCOS[:, :], cosPd)
        t_sinP = gload(TSIN[:, :], sinPd)
        t_msk = gload(TMSK[:, :], maskTd)
        t_id = gload(TID[:, :], identd)
        t_idf = sdma(TIDF[:, :], identd)
        t_sel2 = gload(TSEL2[:, :], sel2d)
        t_sel2t = gload(TSEL2T[:, :], sel2Td)

        t_onesc = gload(TONESC[:, :], onescd)
        t_onesr = gload(TONESR[:, :], onesrd)
        t_crc = gload(CROWC[:, :], cosrow)
        t_crm = gload(CROWM[:, :], msinrow)
        t_onesrf = sdma(TONESRF[:, :], onesrd)
        t_onestf = sdma(TONESTF[:, :], onestd)
        t_gam = sdma(TGAM[:, :], gamma_fm)
        t_bet = sdma(TBET[:, :], beta_fm)
        t_del = sdma(TDELPC[:, :], delta_pc)
        t_qks = sdma(TQKS[:, :], qks)
        t_const = sdma(TCONST[:, :], constvd)
        for i, v in enumerate([0.5, 1.0, 1e-5, 0.125, -0.5, -1.0, 0.0, 2.0,
                               1e-24]):
            nc.const_aps.register(F32, v, TCONST[:, i:i + 1])
        S.setup_barrier = [t_const]

        t_vt0 = S.step("vector", lambda e: e.memset(VT0[:, :], 1.0))
        t_vt1 = S.step("vector", lambda e: e.memset(VT1[:, :], 1.0))

        t = S.step("tensor", lambda e: e.matmul(PS2[:, 0:1], TONESRF[:, :], TQKS[:, :],
                                                start=True, stop=True),
                   deps=[t_onesrf, t_qks])
        t_qkcol = S.step("vector", lambda e: e.tensor_copy(out=QKCOL[:, :],
                                                           in_=PS2[:, 0:1]), deps=[t])

        PI_HI = 3.140625
        PI_LO = float(np.float32(math.pi - PI_HI))
        TWO_PI = float(np.float32(2 * math.pi))
        HALF_PI = float(np.float32(math.pi / 2))
        dcl = CM0[:, 0:NP]
        drr = CM0[:, NP:2 * NP]
        ww = CM0[:, 2 * NP:3 * NP]
        sdr = CM0[:, 3 * NP:4 * NP]
        cdr = CM0[:, 4 * NP:5 * NP]
        t = S.step("vector", lambda e: e.tensor_scalar(out=dcl, in0=TDELPC[:, :],
                                                       scalar1=-TWO_PI, scalar2=0.0,
                                                       op0=ALU.max, op1=ALU.min),
                   deps=[t_del])
        t_dr = S.step("vector", lambda e: e.tensor_scalar(out=drr, in0=dcl,
                                                          scalar1=PI_HI, scalar2=PI_LO,
                                                          op0=ALU.add, op1=ALU.add),
                      deps=[t])
        t_sd = S.step("scalar", lambda e: e.activation(sdr, drr, AF.Sin), deps=[t_dr])
        TP_HI = 6.28125
        TP_LO = float(np.float32(2 * math.pi - TP_HI))
        HP_HI = 1.5703125
        HP_LO = float(np.float32(math.pi / 2 - HP_HI))
        sel = CM0[:, 5 * NP:6 * NP]
        t = S.step("vector", lambda e: e.tensor_scalar(out=sel, in0=drr,
                                                       scalar1=HALF_PI, scalar2=None,
                                                       op0=ALU.is_gt), deps=[t_dr])
        t = S.step("vector", lambda e: e.scalar_tensor_tensor(out=ww, in0=sel,
                                                              scalar=-TP_HI, in1=drr,
                                                              op0=ALU.mult, op1=ALU.add),
                   deps=[t])
        t = S.step("vector", lambda e: e.scalar_tensor_tensor(out=ww, in0=sel,
                                                              scalar=-TP_LO, in1=ww,
                                                              op0=ALU.mult, op1=ALU.add),
                   deps=[t])
        t = S.step("vector", lambda e: e.tensor_scalar(out=ww, in0=ww, scalar1=HP_HI,
                                                       scalar2=HP_LO, op0=ALU.add,
                                                       op1=ALU.add), deps=[t])
        t_cd = S.step("scalar", lambda e: e.activation(cdr, ww, AF.Sin),
                      deps=[t, t_sd])
        t_tc = S.step("vector", lambda e: e.tensor_scalar(out=TRIGC[:, :], in0=cdr,
                                                          scalar1=-1.0, scalar2=None,
                                                          op0=ALU.mult), deps=[t_cd])
        t_ts = S.step("vector", lambda e: e.tensor_scalar(out=TRIGS[:, :], in0=sdr,
                                                          scalar1=-1.0, scalar2=None,
                                                          op0=ALU.mult),
                      deps=[t_cd, t_tc])
        t_trig = [t_tc, t_ts]


        def row_from_cols(dst_row, src_cols, n, deps):
            tt_ = S.step("tensor", lambda e, n=n: e.transpose(
                PS2[0:n, 0:128], src_cols, TIDF[:, :]),
                deps=list(deps) + [t_idf])
            tv = S.step("vector", lambda e, n=n: e.tensor_copy(
                out=TR16[0:n, 0:128], in_=PS2[0:n, 0:128]), deps=[tt_])
            return sdma(dst_row, TR16[0:n, 0:128], deps=[tv])

        def recip_chain(dst, src, deps, scale=-1.0, bias=0.0):
            t1 = S.step("scalar", lambda e: e.activation(dst, src, AF.Ln, bias=bias),
                        deps=deps)
            return S.step("scalar", lambda e: e.activation(dst, dst, AF.Exp,
                                                           scale=scale), deps=[t1])

        def emit_cascade(t1buf, w, out_ap, deps, mbufs=None, prev_st=None):
            m = mbufs if mbufs is not None else [CM1, SXC]
            h = w // 2
            halves = [(0, m[0]), (h, m[1])]
            st = {0: None, 1: None}
            Ccum = 0.0
            for lv, tt in enumerate(CAS_T):
                tau = tt - 128.5
                bias = -(Ccum + tau)
                for hi, (o, mb) in enumerate(halves):
                    if lv == 0:
                        tsg = S.step("scalar", lambda e, bias=bias, o=o, h=h:
                                     e.activation(CSG[:, o:o + h],
                                                  t1buf[:, o:o + h],
                                                  AF.Sign, bias=bias),
                                     deps=list(deps) +
                                     ([prev_st[hi]] if prev_st else []))
                        st[hi] = S.step("vector", lambda e, tt=tt, mb=mb, o=o,
                                        h=h: e.scalar_tensor_tensor(
                            out=mb[:, 0:h], in0=CSG[:, o:o + h],
                            scalar=-tt / 2.0, in1=t1buf[:, o:o + h],
                            op0=ALU.mult, op1=ALU.add), deps=[tsg])
                    else:
                        tsg = S.step("scalar", lambda e, bias=bias, mb=mb, o=o,
                                     h=h: e.activation(CSG[:, o:o + h],
                                                       mb[:, 0:h],
                                                       AF.Sign, bias=bias),
                                     deps=[st[hi]])
                        st[hi] = S.step("vector", lambda e, tt=tt, mb=mb, o=o,
                                        h=h: e.scalar_tensor_tensor(
                            out=mb[:, 0:h], in0=CSG[:, o:o + h],
                            scalar=-tt / 2.0, in1=mb[:, 0:h],
                            op0=ALU.mult, op1=ALU.add), deps=[tsg])
                Ccum += tt / 2.0
            out_tok = None
            for hi, (o, mb) in enumerate(halves):
                out_tok = S.step("vector", lambda e, mb=mb, o=o, h=h:
                                 e.scalar_tensor_tensor(
                    out=out_ap[:, o:o + h], in0=mb[:, 0:h], scalar=-1.0,
                    in1=t1buf[:, o:o + h], op0=ALU.mult, op1=ALU.add),
                    deps=[st[hi]])
            return out_tok, dict(st)

        def emit_ln(src_hbm, phase_deps, gamma_mode=False, apply=True):
            pd = _flat(phase_deps)
            per_chunk = (len(pd) == NCH and
                         all(t[0].startswith("sync") for t in pd))
            ltoks = []
            for cc in range(NCH):
                if per_chunk:
                    d = [pd[cc]]
                elif cc == 0:
                    d = pd
                else:
                    d = ()
                ltoks.append(gload(ACT1[:, cc * T:(cc + 1) * T],
                                   src_hbm[cc * 128:(cc + 1) * 128, :], deps=d))
            sq_toks = []
            for cc in range(NCH):
                sq_toks.append(S.step("vector", lambda e, cc=cc: e.tensor_mul(
                    out=OALL[:, cc * T:(cc + 1) * T],
                    in0=ACT1[:, cc * T:(cc + 1) * T],
                    in1=ACT1[:, cc * T:(cc + 1) * T]),
                    deps=[ltoks[cc]] + (list(phase_deps) if cc == 0 else [])))
            t_sq = sq_toks[-1]
            ev = []
            for g in range(4):
                d1 = (ltoks + [t_onesc] + list(phase_deps)) if g == 0 \
                    else [ev[g - 1][0]]
                tk = None
                for cc in range(NCH):
                    tk = S.step("tensor", lambda e, g=g, cc=cc: e.matmul(
                        PS1[0:1, 512 * (g % 2):512 * (g % 2) + 512], TONESC[:, :],
                        ACT1[:, cc * T + 512 * g:cc * T + 512 * g + 512],
                        start=(cc == 0), stop=(cc == NCH - 1)),
                        deps=(d1 if cc == 0 else []))
                tk2 = None
                for cc in range(NCH):
                    if g == 0:
                        d2 = [sq_toks[cc]]
                    else:
                        d2 = [ev[g - 1][1]] if cc == 0 else []
                    tk2 = S.step("tensor", lambda e, g=g, cc=cc: e.matmul(
                        PS2[0:1, 512 * (g % 2):512 * (g % 2) + 512], TONESC[:, :],
                        OALL[:, cc * T + 512 * g:cc * T + 512 * g + 512],
                        start=(cc == 0), stop=(cc == NCH - 1)), deps=d2)
                e1 = S.step("scalar", lambda e, g=g: e.copy(
                    SROW[0:1, 512 * g:512 * g + 512],
                    PS1[0:1, 512 * (g % 2):512 * (g % 2) + 512]), deps=[tk])
                e2 = S.step("scalar", lambda e, g=g: e.copy(
                    SROW2[0:1, 512 * g:512 * g + 512],
                    PS2[0:1, 512 * (g % 2):512 * (g % 2) + 512]), deps=[tk2])
                ev.append((e1, e2))
            bt = []
            v2_of_bank = {}
            for g in range(4):
                e1, e2 = ev[g]
                b1 = S.step("tensor", lambda e, g=g: e.matmul(
                    PQ[:, 512 * g:512 * g + 512], TONESRF[:, :],
                    SROW[0:1, 512 * g:512 * g + 512], start=True, stop=True),
                    deps=[e1, t_onesrf])
                v1 = S.step("vector", lambda e, g=g: e.tensor_scalar(
                    out=CM0[:, 512 * g:512 * g + 512], in0=PQ[:, 512 * g:512 * g + 512],
                    scalar1=1.0 / C, scalar2=None, op0=ALU.mult), deps=[b1])
                b2 = S.step("tensor", lambda e, g=g: e.matmul(
                    PS1[:, 512 * (g % 2):512 * (g % 2) + 512], TONESRF[:, :],
                    SROW2[0:1, 512 * g:512 * g + 512], start=True, stop=True),
                    deps=[e2] + ([ev[3][0], ev[3][1]] if g == 0 else []) +
                    ([v2_of_bank[g % 2]] if (g % 2) in v2_of_bank else []))
                v2 = S.step("vector", lambda e, g=g: e.tensor_scalar(
                    out=CM1[:, 512 * g:512 * g + 512],
                    in0=PS1[:, 512 * (g % 2):512 * (g % 2) + 512],
                    scalar1=1.0 / C, scalar2=None, op0=ALU.mult), deps=[b2])
                v2_of_bank[g % 2] = v2
                bt += [v1, v2]
            t_m2 = S.step("scalar", lambda e: e.activation(SXC[:, :], CM0[:, :],
                                                           AF.Square), deps=bt)
            t_var = S.step("vector", lambda e: e.tensor_sub(out=CM1[:, :], in0=CM1[:, :],
                                                            in1=SXC[:, :]), deps=[t_m2])
            if not gamma_mode:
                t_r = recip_chain(SXC[:, :], CM1[:, :], [t_var], scale=-0.5,
                                  bias=LN_EPS)
            else:
                t1 = S.step("scalar", lambda e: e.activation(SXC[:, :], CM1[:, :],
                                                             AF.Ln, bias=LN_EPS),
                            deps=[t_var])
                t2 = S.step("scalar", lambda e: e.activation(SXO1[:, :], SXC[:, :],
                                                             AF.Exp, scale=-1.0),
                            deps=[t1])
                t3 = S.step("vector", lambda e: e.tensor_mul(out=SXO1[:, :],
                                                             in0=SXO1[:, :],
                                                             in1=CM1[:, :]), deps=[t2])
                t4 = S.step("scalar", lambda e: e.activation(SXO1[:, :], SXO1[:, :],
                                                             AF.Ln, bias=LN_EPS),
                            deps=[t3])
                t5 = S.step("vector", lambda e: e.tensor_add(out=SXC[:, :],
                                                             in0=SXC[:, :],
                                                             in1=SXO1[:, :]), deps=[t4])
                t_r = S.step("scalar", lambda e: e.activation(SXC[:, :], SXC[:, :],
                                                              AF.Exp, scale=-0.5),
                             deps=[t5])
            if not apply:
                return [t_r]
            t_rb = S.step("vector", lambda e: e.tensor_copy(out=SB3[:, :],
                                                            in_=SXC[:, :]), deps=[t_r])
            t_bb = S.step("vector", lambda e: e.scalar_tensor_tensor(
                out=SB4[:, :], in0=CM0[:, :], scalar=-1.0, in1=SXC[:, :],
                op0=ALU.mult, op1=ALU.mult), deps=[t_r])
            atoks = []
            prevc = None
            for cc in range(NCH):
                ta1 = S.step("vector", lambda e, cc=cc: e.tensor_mul(
                    out=ACT1[:, cc * T:(cc + 1) * T], in0=ACT1[:, cc * T:(cc + 1) * T],
                    in1=SB3[:, :]),
                    deps=[t_rb, sq_toks[cc]] + ([prevc] if prevc else []))
                tb2 = S.step("vector", lambda e, cc=cc: e.tensor_add(
                    out=ACT1[:, cc * T:(cc + 1) * T], in0=ACT1[:, cc * T:(cc + 1) * T],
                    in1=SB4[:, :]), deps=[t_bb, ta1])
                prevc = tb2
                atoks.append(tb2)
            return atoks

        def emit_attention(layer, prep_pair, pair_end, exp_scale_imm, exp_scale_ap,
                           dep_start, body_maker=None, aux=None):
            aux = aux if aux is not None else []
            evacs = []
            prev = list(dep_start)
            S.attn_sums_gate = None
            exp_of = {0: None, 1: None}
            av_of = {0: None, 1: None}
            bank_gate = {0: None, 1: None}
            sums_gate = [None]
            nthunks = []
            fin_next = [None]
            if body_maker is not None:
                th0, fin0 = body_maker(0, {"start": dep_start,
                                           "rope": list(dep_start),
                                           "pq23": None, "sb2": [], "vt": []})
                for t in th0:
                    t()
                fin_next[0] = fin0
            for p8 in range(NP):
                if body_maker is None:
                    tprep, qs_t, ks_t, vt, vt_tok = prep_pair(p8, prev)
                else:
                    (tprep, qs_t, ks_t, vt, vt_tok, ropetoks, pq23tok,
                     lastmm) = fin_next[0](prev)
                    if p8 + 1 < NP:
                        nthunks, fin_next[0] = body_maker(
                            p8 + 1, {"start": [], "rope": ropetoks,
                                     "pq23": pq23tok, "sb2": list(vt_tok),
                                     "vt": [S.tok("tensor")],
                                     "wq_free": lastmm})
                    else:
                        nthunks = []
                if DBG and layer == "a" and p8 == 0:
                    tp0 = (tprep[1] if isinstance(tprep[0], tuple)
                           else tprep)
                    dl = [sdma(dbg["dbgqs0"], QS0[:, :], deps=[tp0]),
                          sdma(dbg["dbgqs1"], QS1[:, :], deps=[tp0]),
                          sdma(dbg["dbgks0"], KS0[:, :], deps=[tp0]),
                          sdma(dbg["dbgks1"], KS1[:, :], deps=[tp0]),
                          sdma(dbg["dbgvt0"], vt[:, :], deps=[tp0]),
                          sdma(dbg["dbgsb0"], SB0[:, :], deps=[tp0]),
                          sdma(dbg["dbgsb1"], SB1[:, :], deps=[tp0]),
                          sdma(dbg["dbgsb2"], SB2[:, :], deps=[tp0])]
                    S.step("vector", lambda e: e.engine_nop(), deps=dl)
                for hh in range(2):
                    h = 2 * p8 + hh
                    QSt, KSt = qs_t[hh], ks_t[hh]
                    lo, hi = 0, 128
                    pairs = []
                    jg = 0
                    for g in range(4):
                        njs = 4 * g + 4
                        for j0 in range(0, njs, 2):
                            pairs.append((g, [j0, j0 + 1], jg % 2, njs))
                            jg += 1
                    ev_g = {}
                    sr_g = {}

                    def emit_avs(pend, vt=vt, hh=hh, h=h):
                        g, jp, sb, njs, ex, ATT = pend
                        bk = g % 2
                        last_av = None
                        for ji, j in enumerate(jp):
                            off = max(0, 128 * j - 512 * g)
                            n = 512 - off
                            first = (j == 0)
                            last = (j == njs - 1)
                            d = [ex]
                            if first and g == 0:
                                d += list(vt_tok)
                            if first and bank_gate[bk] is not None:
                                d += bank_gate[bk]
                            last_av = S.step("tensor", lambda e, j=j, ji=ji,
                                             off=off, n=n, bk=bk, first=first,
                                             last=last, ATT=ATT, vt=vt, hh=hh:
                                             e.matmul(
                                PQ[0:65, 512 * bk + off:512 * bk + off + n],
                                vt[:, (16 * hh + j) * 80:(16 * hh + j) * 80 + 65],
                                ATT[:, 512 * ji + off:512 * ji + off + n],
                                start=first, stop=last), deps=d)
                        av_of[sb] = last_av
                        if jp[-1] == njs - 1:
                            g_ = g
                            ev_g[g_] = S.step("vector", lambda e, g=g_, bk=bk,
                                              h=h: e.tensor_copy(
                                out=OALL[64 * (h % 2):64 * (h % 2) + 64,
                                         (h // 2) * T + 512 * g:
                                         (h // 2) * T + 512 * g + 512],
                                in_=PQ[0:64, 512 * bk:512 * bk + 512]),
                                deps=[last_av])
                            sr_g[g_] = S.step("vector", lambda e, g=g_, bk=bk:
                                              e.tensor_copy(
                                out=SROW[0:1, 512 * g:512 * g + 512],
                                in_=PQ[64:65, 512 * bk:512 * bk + 512]),
                                deps=[last_av] +
                                ([sums_gate[0]] if (g_ == 0 and sums_gate[0])
                                 else []))
                            bank_gate[bk] = [ev_g[g_], sr_g[g_]]

                    pend = None
                    for pi, (g, jp, sb, njs) in enumerate(pairs):
                        SC = PS1 if sb == 0 else PS2
                        ATT = ATT0 if sb == 0 else ATT1
                        mm_tok = None
                        for ji, j in enumerate(jp):
                            off = max(0, 128 * j - 512 * g)
                            n = 512 - off
                            qlo = 512 * g + off
                            diag = (j >= 4 * g)
                            d = []
                            if ji == 0:
                                d = [tprep[hh]
                                     if isinstance(tprep[0], tuple)
                                     else tprep]
                                if exp_of[sb] is not None:
                                    d.append(exp_of[sb])
                            mm_tok = S.step("tensor", lambda e, j=j, ji=ji, off=off,
                                            n=n, qlo=qlo, diag=diag, SC=SC, QSt=QSt,
                                            KSt=KSt, lo=lo, hi=hi: e.matmul(
                                SC[:, 512 * ji + off:512 * ji + off + n],
                                KSt[lo:hi, 128 * j:128 * j + 128],
                                QSt[lo:hi, qlo:qlo + n],
                                start=True, stop=(not diag)), deps=d)
                            if diag:
                                mm_tok = S.step("tensor", lambda e, ji=ji, off=off,
                                                SC=SC: e.matmul(
                                    SC[:, 512 * ji + off:512 * ji + off + 128],
                                    TID[:, :], TMSK[:, :], start=False, stop=True),
                                    deps=[])
                        ed = [mm_tok]
                        off0 = max(0, 128 * jp[0] - 512 * g)
                        if exp_scale_ap is not None:
                            ex = S.step("scalar", lambda e, SC=SC, ATT=ATT,
                                        off0=off0:
                                        e.activation(ATT[:, off0:1024],
                                                     SC[:, off0:1024], AF.Exp,
                                                     scale=exp_scale_ap),
                                        deps=ed)
                        else:
                            ex = S.step("scalar", lambda e, SC=SC, ATT=ATT,
                                        off0=off0:
                                        e.activation(ATT[:, off0:1024],
                                                     SC[:, off0:1024], AF.Exp,
                                                     scale=exp_scale_imm),
                                        deps=ed)
                        exp_of[sb] = ex
                        if (DBG and layer == "a" and p8 == 0 and hh == 0
                                and pi in (0, 1)):
                            exp_of[sb] = sdma(dbg["dbgatt%d" % pi],
                                              ATT[:, :], deps=[ex])
                        if pend is not None:
                            emit_avs(pend)
                            if nthunks:
                                nthunks.pop(0)()
                            elif aux:
                                aux.pop(0)()
                        pend = (g, jp, sb, njs, ex, ATT)
                    emit_avs(pend)
                    while nthunks:
                        nthunks.pop(0)()
                    tpl = S.step("sync", lambda e, h=h: e.dma_start(
                        out=SUMS[h:h + 1, :], in_=SROW[0:1, :]), dma=True,
                        deps=[sr_g[gg] for gg in range(4)])
                    sums_gate[0] = tpl
                    S.attn_sums_gate = tpl
                    evacs.append((ev_g[3], tpl))
                    if DBG and layer == "a" and p8 == 0 and hh == 1:
                        dob = sdma(dbg["dbgoall"], OALL[:, 0:T],
                                   deps=[ev_g[3], evacs[-2][0]])
                        evacs.append((dob, dob))
                if pair_end is not None:
                    pair_end(p8)
                prev = [S.tok("tensor"), S.tok("scalar"), S.tok("vector")]
            while aux:
                aux.pop(0)()
            return evacs

        def emit_norm_oall(evacs):
            sum_deps = [e[1] for e in evacs] + [e[0] for e in evacs] + \
                [S.tok("vector"), S.tok("tensor")]
            t_r = recip_chain(SUMS[:, :], SUMS[:, :], sum_deps, scale=-1.0)
            t_lh = sdma(TSELHF[:, :], selhd, deps=sum_deps)
            toks = []
            if DBG:
                toks.append(sdma(dbg["dbgsums"], SUMS[:, :], deps=[t_r]))
            for p8 in range(NP):
                bs = []
                for g in range(4):
                    d = [t_r, t_lh] + [e[0] for e in evacs] if (p8 == 0 and g == 0) \
                        else ([toks[-1]] if (g == 0 and toks) else [])
                    bs.append(S.step("tensor", lambda e, p8=p8, g=g: e.matmul(
                        PQ[:, 512 * g:512 * g + 512],
                        TSELHF[:, p8 * 128:(p8 + 1) * 128],
                        SUMS[:, 512 * g:512 * g + 512],
                        start=True, stop=True), deps=d))
                tm = S.step("vector", lambda e, p8=p8: e.tensor_mul(
                    out=OALL[:, p8 * T:(p8 + 1) * T], in0=OALL[:, p8 * T:(p8 + 1) * T],
                    in1=PQ[:, 0:2048]), deps=[bs[-1], bs[0]])
                toks.append(tm)
            return toks

        def emit_proj(w_hbm, src_hbm, dst_hbm, phase_deps):
            wtok = [None, None]

            def loadw(oc, deps):
                b = WPB0 if oc % 2 == 0 else WPB1
                tks = []
                for cc in range(NCH):
                    tks.append(gload(
                        b[:, cc * 128:(cc + 1) * 128],
                        w_hbm[cc * 128:(cc + 1) * 128, oc * 128:(oc + 1) * 128],
                        deps=(deps if cc == 0 else ())))
                wtok[oc % 2] = tks

            loadw(0, phase_deps)
            loadw(1, phase_deps)
            evac_tok = [None, None]
            last_store = None
            st_toks = []
            for oc in range(NCH):
                b = WPB0 if oc % 2 == 0 else WPB1
                mm = None
                for tg in range(4):
                    for cc in range(NCH):
                        d = []
                        if tg == 0 and cc == 0:
                            d = [wtok[oc % 2]] + list(phase_deps)
                            if evac_tok[oc % 2] is not None:
                                d.append(evac_tok[oc % 2])
                        if oc % 2 == 0:
                            mm = S.step("tensor", lambda e, tg=tg, cc=cc, b=b: e.matmul(
                                PQ[:, 512 * tg:512 * tg + 512],
                                b[:, cc * 128:(cc + 1) * 128],
                                OALL[:, cc * T + 512 * tg:cc * T + 512 * tg + 512],
                                start=(cc == 0), stop=(cc == NCH - 1)), deps=d)
                        else:
                            PSx = PS1 if tg < 2 else PS2
                            mm = S.step("tensor", lambda e, tg=tg, cc=cc, b=b, PSx=PSx:
                                        e.matmul(
                                PSx[:, 512 * (tg % 2):512 * (tg % 2) + 512],
                                b[:, cc * 128:(cc + 1) * 128],
                                OALL[:, cc * T + 512 * tg:cc * T + 512 * tg + 512],
                                start=(cc == 0), stop=(cc == NCH - 1)), deps=d)
                if oc + 2 < NCH:
                    loadw(oc + 2, [mm])
                xbuf = SXO0 if oc % 2 == 0 else SXO1
                t_old = sdma(xbuf[:, :], src_hbm[oc * 128:(oc + 1) * 128, :],
                             deps=([evac_tok[oc % 2]] if evac_tok[oc % 2] else
                                   list(phase_deps)))
                if oc % 2 == 0:
                    t_add = S.step("vector", lambda e, xbuf=xbuf: e.tensor_add(
                        out=SXC[:, :], in0=PQ[:, 0:2048], in1=xbuf[:, :]),
                        deps=[mm, t_old] + ([last_store] if last_store else []))
                else:
                    ta1 = S.step("vector", lambda e, xbuf=xbuf: e.tensor_add(
                        out=SXC[:, 0:1024], in0=PS1[:, :], in1=xbuf[:, 0:1024]),
                        deps=[mm, t_old] + ([last_store] if last_store else []))
                    t_add = S.step("vector", lambda e, xbuf=xbuf: e.tensor_add(
                        out=SXC[:, 1024:2048], in0=PS2[:, :], in1=xbuf[:, 1024:2048]),
                        deps=[ta1])
                evac_tok[oc % 2] = t_add
                t_st = S.step("sync", lambda e, oc=oc: e.dma_start(
                    out=dst_hbm[oc * 128:(oc + 1) * 128, :], in_=SXC[:, :]),
                    dma=True, deps=[t_add])
                last_store = t_st
                st_toks.append(t_st)
            return st_toks

        def emit_rhs_prep_scales(deps):
            am_toks = []
            for oc in range(NCH):
                tl = sdma(RSTG[:, :], rhsT[oc * 128:(oc + 1) * 128, :],
                          deps=(list(deps) if oc == 0 else [am_toks[-1]]))
                tr = S.step("vector", lambda e, oc=oc: e.tensor_reduce(
                    out=AMX[:, oc:oc + 1], in_=RSTG[:, :], axis=AX.X, op=ALU.max,
                    apply_absolute_value=True), deps=[tl])
                am_toks.append(tr)
            t0 = S.step("vector", lambda e: e.tensor_scalar(
                out=AMX[:, 0:NCH], in0=AMX[:, 0:NCH], scalar1=1e-12, scalar2=None,
                op0=ALU.max), deps=am_toks)
            t1 = S.step("vector", lambda e: e.tensor_scalar(
                out=SCCOL[:, :], in0=AMX[:, 0:NCH], scalar1=1.0 / 127.0, scalar2=None,
                op0=ALU.mult), deps=[t0])
            t2 = recip_chain(AMXS[:, 0:NCH], AMX[:, 0:NCH], [t1], scale=-1.0)
            t3 = S.step("vector", lambda e: e.tensor_scalar(
                out=AMXS[:, 0:NCH], in0=AMXS[:, 0:NCH], scalar1=127.0, scalar2=None,
                op0=ALU.mult), deps=[t2])
            rowd = row_from_cols(SXO1[0:1, 0:C], AMXS[:, 0:NCH], NCH, [t3])
            bca = []
            for g in range(2):
                b1 = S.step("tensor", lambda e, g=g: e.matmul(
                    PS1[:, 512 * g:512 * g + 512], TONESRF[:, :],
                    SXO1[0:1, 512 * g:512 * g + 512],
                    start=True, stop=True), deps=[rowd, t_onesrf])
                bca.append(S.step("vector", lambda e, g=g: e.tensor_copy(
                    out=SXO1[:, 512 * g:512 * g + 512],
                    in_=PS1[:, 512 * g:512 * g + 512]), deps=[b1]))
            return [bca[-1], t1], am_toks[-1]

        rhs_qs = {"qtok": None}

        def make_rhs_q_thunks(sc_toks, am_last):
            thunks = []
            ctx = {"tm_prev": None, "tq_prev": {}, "pst": None}
            MB = [SXO1[:, 1024:1536], SXO1[:, 1536:2048]]
            for cc in range(NCH):
                t1b = SXO0[:, 0:1024] if cc % 2 == 0 else SXO0[:, 1024:2048]

                def th_lm(cc=cc, t1b=t1b):
                    dld = [am_last] if cc == 0 else []
                    if (cc % 2) in ctx["tq_prev"]:
                        dld.append(ctx["tq_prev"][cc % 2])
                    tl = sdma(t1b[:, 0:C], rhs[cc * 128:(cc + 1) * 128, :],
                              deps=dld)
                    ctx["tm_prev"] = S.step("vector", lambda e, t1b=t1b:
                                            e.tensor_mul(
                        out=t1b[:, 0:C], in0=t1b[:, 0:C],
                        in1=SXO1[:, 0:C]),
                        deps=[tl] + list(sc_toks))
                    ctx["lvst"] = {0: None, 1: None}
                    ctx["Ccum"] = 0.0
                thunks.append(th_lm)

                def th_lv(lv0, cc=cc, t1b=t1b):
                    def f():
                        h = 512
                        for lv in (lv0, lv0 + 1):
                            tt = CAS_T[lv]
                            bias = -(ctx["Ccum"] + tt - 128.5)
                            for hi in range(2):
                                o = hi * h
                                mb = MB[hi]
                                if lv == 0:
                                    tsg = S.step("scalar",
                                                 lambda e, bias=bias, o=o:
                                                 e.activation(
                                        CSG[:, o:o + h], t1b[:, o:o + h],
                                        AF.Sign, bias=bias),
                                        deps=[ctx["tm_prev"]] +
                                        ([ctx["pst"][hi]] if ctx["pst"]
                                         else []))
                                    ctx["lvst"][hi] = S.step(
                                        "vector", lambda e, tt=tt, mb=mb, o=o:
                                        e.scalar_tensor_tensor(
                                            out=mb[:, 0:h],
                                            in0=CSG[:, o:o + h],
                                            scalar=-tt / 2.0,
                                            in1=t1b[:, o:o + h],
                                            op0=ALU.mult, op1=ALU.add),
                                        deps=[tsg])
                                else:
                                    tsg = S.step("scalar",
                                                 lambda e, bias=bias, o=o,
                                                 mb=mb: e.activation(
                                        CSG[:, o:o + h], mb[:, 0:h],
                                        AF.Sign, bias=bias),
                                        deps=[ctx["lvst"][hi]])
                                    ctx["lvst"][hi] = S.step(
                                        "vector", lambda e, tt=tt, mb=mb, o=o:
                                        e.scalar_tensor_tensor(
                                            out=mb[:, 0:h],
                                            in0=CSG[:, o:o + h],
                                            scalar=-tt / 2.0,
                                            in1=mb[:, 0:h],
                                            op0=ALU.mult, op1=ALU.add),
                                        deps=[tsg])
                            ctx["Ccum"] += tt / 2.0
                    return f
                for lv0 in (0, 2, 4, 6):
                    thunks.append(th_lv(lv0))

                def th_fin(cc=cc, t1b=t1b):
                    ot = None
                    for hi in range(2):
                        o = hi * 512
                        mb = MB[hi]
                        ot = S.step("vector", lambda e, mb=mb, o=o, cc=cc:
                                    e.scalar_tensor_tensor(
                            out=QRHS[:, cc * C + o:cc * C + o + 512],
                            in0=mb[:, 0:512], scalar=-1.0,
                            in1=t1b[:, o:o + 512],
                            op0=ALU.mult, op1=ALU.add),
                            deps=[ctx["lvst"][hi]])
                    ctx["pst"] = dict(ctx["lvst"])
                    tq = S.step("vector", lambda e, cc=cc: e.tensor_scalar(
                        out=QRHS[:, cc * C:(cc + 1) * C],
                        in0=QRHS[:, cc * C:(cc + 1) * C],
                        scalar1=-0.5, scalar2=None, op0=ALU.add), deps=[ot])
                    ctx["tq_prev"][cc % 2] = tq
                    rhs_qs["qtok"] = tq
                thunks.append(th_fin)
            return thunks

        def emit_rhs_colsums(deps):
            cs_last = None
            for half in range(2):
                for cc in range(NCH):
                    cs_last = S.step("tensor", lambda e, cc=cc, half=half: e.matmul(
                        PS1[0:1, 512 * half:512 * half + 512], TONESC[:, :],
                        QRHS[:, cc * C + 512 * half:cc * C + 512 * half + 512],
                        start=(cc == 0), stop=(cc == NCH - 1)),
                        deps=([rhs_qs["qtok"], t_onesc] + list(deps)
                              if (cc == 0 and half == 0) else []))
            tev = S.step("vector", lambda e: e.tensor_copy(out=BRH[0:1, 0:C],
                                                           in_=PS1[0:1, 0:1024]),
                         deps=[cs_last])
            ctoks = []
            for oc in range(NCH):
                tc1 = S.step("tensor", lambda e, oc=oc: e.matmul(
                    PS2[:, 0:1], BRH[0:1, oc * 128:(oc + 1) * 128], TONESTF[:, :],
                    start=True, stop=True),
                    deps=[tev, t_onestf] + ([ctoks[-1]] if ctoks else []))
                ctoks.append(S.step("vector", lambda e, oc=oc: e.tensor_scalar(
                    out=CORRC[:, oc:oc + 1], in0=PS2[:, 0:1],
                    scalar1=-0.5, scalar2=None, op0=ALU.mult), deps=[tc1]))
            return [rhs_qs["qtok"], ctoks[-1]]

        def emit_ffn(src_hbm, dst_hbm, ln_toks, rhs_ready):
            tmx = S.step("vector", lambda e: e.memset(AMX[:, :], 0.0),
                         deps=list(ln_toks))
            tw = None
            prev_ttr = None
            prev_trd = None
            for cc in range(NCH):
                tld = sdma(SXO0[:, :], src_hbm[cc * 128:(cc + 1) * 128, :],
                           deps=(list(ln_toks) if cc == 0 else [tw]))
                tw = S.step("vector", lambda e: e.tensor_sub(
                    out=CM1[:, :], in0=SXO0[:, :], in1=CM0[:, :]),
                    deps=[tld] + (list(ln_toks) if cc == 0 else [prev_ttr]))
                pq3 = PQ[:, 0:2048].rearrange(
                    "p (b n) -> p b n", b=4)[:, :, 0:128]
                trds = []
                for r_ in range(4):
                    for b in range(4):
                        tb_ = 4 * r_ + b
                        d = []
                        if b == 0:
                            d = [tw, t_idf] + ([trds[-1]] if trds
                                               else ([prev_trd]
                                                     if prev_trd else []))
                        prev_ttr = S.step("tensor", lambda e, tb_=tb_, b=b:
                                          e.transpose(
                            PQ[:, 512 * b:512 * b + 128],
                            CM1[:, 128 * tb_:128 * tb_ + 128], TIDF[:, :]),
                            deps=d)
                    trds.append(S.step("vector", lambda e, r_=r_, pq3=pq3:
                                       e.tensor_reduce(
                        out=AMXS[:, 4 * r_:4 * r_ + 4], in_=pq3,
                        axis=AX.X, op=ALU.max, apply_absolute_value=True),
                        deps=[prev_ttr]))
                tmx = S.step("vector", lambda e: e.tensor_tensor(
                    out=AMX[:, :], in0=AMX[:, :], in1=AMXS[:, :], op=ALU.max),
                    deps=trds + [tmx])
                prev_trd = trds[-1]
            rowd = row_from_cols(SXO1[0:1, 0:T], AMX[:, 0:16], 16, [tmx])
            rmu = S.step("vector", lambda e: e.tensor_mul(
                out=SXO1[0:1, 0:T], in0=SXO1[0:1, 0:T], in1=SXC[0:1, 0:T]),
                deps=[rowd] + list(ln_toks))
            rcl = S.step("vector", lambda e: e.tensor_scalar(
                out=SXO1[0:1, 0:T], in0=SXO1[0:1, 0:T], scalar1=1e-12,
                scalar2=None, op0=ALU.max), deps=[rmu])
            rrec = recip_chain(CM1[0:1, 0:T], SXO1[0:1, 0:T], [rcl], scale=-1.0)
            r127 = S.step("vector", lambda e: e.tensor_scalar(
                out=CM1[0:1, 0:T], in0=CM1[0:1, 0:T], scalar1=127.0,
                scalar2=None, op0=ALU.mult), deps=[rrec])
            rqb = []
            for g in range(4):
                b1 = S.step("tensor", lambda e, g=g: e.matmul(
                    PQ[:, 512 * g:512 * g + 512], TONESRF[:, :],
                    CM1[0:1, 512 * g:512 * g + 512], start=True, stop=True),
                    deps=[r127, t_onesrf])
                rqb.append(S.step("vector", lambda e, g=g: e.tensor_copy(
                    out=AR2[:, 512 * g:512 * g + 512],
                    in_=PQ[:, 512 * g:512 * g + 512]), deps=[b1]))
            ta2 = S.step("vector", lambda e: e.tensor_mul(
                out=AR2[:, :], in0=AR2[:, :], in1=SXC[:, :]),
                deps=rqb + list(ln_toks))
            tb2 = S.step("vector", lambda e: e.scalar_tensor_tensor(
                out=CM0[:, :], in0=CM0[:, :], scalar=-1.0, in1=AR2[:, :],
                op0=ALU.mult, op1=ALU.mult), deps=[ta2])
            slr = S.step("vector", lambda e: e.tensor_scalar(
                out=SXC[0:1, 0:T], in0=SXO1[0:1, 0:T], scalar1=1.0 / 127.0,
                scalar2=None, op0=ALU.mult), deps=[ta2, rcl])
            tq = None
            tq_prev = {}
            pst = None
            for cc in range(NCH):
                t1b = SXO0 if cc % 2 == 0 else CM1
                if cc == 0:
                    dld = [tw]
                elif cc == 1:
                    dld = [slr]
                else:
                    dld = [tq_prev[cc % 2]]
                tld = sdma(t1b[:, :], src_hbm[cc * 128:(cc + 1) * 128, :],
                           deps=dld)
                t_a = S.step("gpsimd", lambda e, t1b=t1b: e.tensor_tensor(
                    out=t1b[:, :], in0=t1b[:, :], in1=AR2[:, :],
                    op=ALU.mult), deps=[tld, tb2, slr])
                t_b = S.step("gpsimd", lambda e, t1b=t1b: e.tensor_tensor(
                    out=t1b[:, :], in0=t1b[:, :], in1=CM0[:, :],
                    op=ALU.add), deps=[t_a])
                tq, pst = emit_cascade(t1b, T, ACT1[:, cc * T:(cc + 1) * T],
                                       [t_b],
                                       mbufs=[SXO1[:, 0:1024],
                                              SXO1[:, 1024:2048]],
                                       prev_st=pst)
                tq_prev[cc % 2] = tq
            slb = []
            for g in range(4):
                PSx = PS1 if g < 2 else PS2
                b1 = S.step("tensor", lambda e, g=g, PSx=PSx: e.matmul(
                    PSx[:, 512 * (g % 2):512 * (g % 2) + 512], TONESRF[:, :],
                    SXC[0:1, 512 * g:512 * g + 512], start=True, stop=True),
                    deps=[slr, t_onesrf])
                slb.append(S.step("vector", lambda e, g=g, PSx=PSx: e.tensor_copy(
                    out=AR2[:, 512 * g:512 * g + 512],
                    in_=PSx[:, 512 * (g % 2):512 * (g % 2) + 512]),
                    deps=[b1, tq]))
            evac_tok = [None, None]
            last_store = None
            st_toks = []
            for oc in range(NCH):
                mm = None
                for tg in range(4):
                    for cc in range(NCH):
                        d = []
                        if tg == 0 and cc == 0:
                            d = [tq] + (list(rhs_ready) if oc == 0 else [])
                            if oc == 1:
                                d.append(slb[-1])
                            if evac_tok[oc % 2] is not None:
                                d.append(evac_tok[oc % 2])
                        if oc % 2 == 0:
                            mm = S.step("tensor", lambda e, oc=oc, tg=tg, cc=cc:
                                        e.matmul(
                                PQ[:, 512 * tg:512 * tg + 512],
                                QRHS[:, cc * C + oc * 128:cc * C + oc * 128 + 128],
                                ACT1[:, cc * T + 512 * tg:cc * T + 512 * tg + 512],
                                start=(cc == 0), stop=(cc == NCH - 1)), deps=d)
                        else:
                            PSx = PS1 if tg < 2 else PS2
                            mm = S.step("tensor", lambda e, oc=oc, tg=tg, cc=cc,
                                        PSx=PSx: e.matmul(
                                PSx[:, 512 * (tg % 2):512 * (tg % 2) + 512],
                                QRHS[:, cc * C + oc * 128:cc * C + oc * 128 + 128],
                                ACT1[:, cc * T + 512 * tg:cc * T + 512 * tg + 512],
                                start=(cc == 0), stop=(cc == NCH - 1)), deps=d)
                xbuf = SXO0 if oc % 2 == 0 else SXO1
                t_old = sdma(xbuf[:, :], src_hbm[oc * 128:(oc + 1) * 128, :],
                             deps=([evac_tok[oc % 2]] if evac_tok[oc % 2]
                                   else [tq]))
                dls = [last_store] if last_store else []
                if oc % 2 == 0:
                    td1 = S.step("vector", lambda e, oc=oc: e.tensor_scalar(
                        out=CM0[:, :], in0=PQ[:, 0:2048],
                        scalar1=CORRC[:, oc:oc + 1], scalar2=SCCOL[:, oc:oc + 1],
                        op0=ALU.add, op1=ALU.mult), deps=[mm, t_old] + dls)
                else:
                    ta_ = S.step("vector", lambda e, oc=oc: e.tensor_scalar(
                        out=CM0[:, 0:1024], in0=PS1[:, :],
                        scalar1=CORRC[:, oc:oc + 1], scalar2=SCCOL[:, oc:oc + 1],
                        op0=ALU.add, op1=ALU.mult), deps=[mm, t_old] + dls)
                    td1 = S.step("vector", lambda e, oc=oc: e.tensor_scalar(
                        out=CM0[:, 1024:2048], in0=PS2[:, :],
                        scalar1=CORRC[:, oc:oc + 1], scalar2=SCCOL[:, oc:oc + 1],
                        op0=ALU.add, op1=ALU.mult), deps=[ta_])
                td2 = S.step("gpsimd", lambda e: e.tensor_tensor(
                    out=CM0[:, :], in0=CM0[:, :], in1=AR2[:, :],
                    op=ALU.mult), deps=[td1, slb[-1]])
                t_add = S.step("vector", lambda e, xbuf=xbuf: e.tensor_add(
                    out=SXC[:, :], in0=CM0[:, :], in1=xbuf[:, :]), deps=[td2])
                evac_tok[oc % 2] = t_add
                t_st = S.step("sync", lambda e, oc=oc: e.dma_start(
                    out=dst_hbm[oc * 128:(oc + 1) * 128, :], in_=SXC[:, :]),
                    dma=True, deps=[t_add])
                last_store = t_st
                st_toks.append(t_st)
            return st_toks

        wa_tok = [None]

        def load_wa(p8, deps):
            tks = []
            for cc in range(NCH):
                for part in range(3):
                    tks.append(gload(
                        WA[:, cc * 384 + part * 128:cc * 384 + part * 128 + 128],
                        waT[cc * 128:(cc + 1) * 128,
                            part * C + p8 * 128:part * C + p8 * 128 + 128],
                        deps=(deps if (cc == 0 and part == 0) else ())))
            wa_tok[0] = tks

        def prep_a_body(p8, gates):
            vt = VT0 if p8 % 2 == 0 else VT1
            wtok = wa_tok[0]
            thunks = []
            mus = {0: [], 1: []}
            tvt = []
            bank_free = {0: gates.get("pq23"), 1: gates.get("pq23")}
            cm_gate = list(gates.get("rope") or [])
            sb2_gate = list(gates.get("sb2") or [])
            vt_gate = list(gates.get("vt") or [])
            start_gate = list(gates.get("start") or [])
            mm_done = {}
            last_mm = [None]

            def th_mm(part, half, tgl):
                def f():
                    tg = 2 * half + tgl
                    mm = None
                    for cc in range(NCH):
                        d = []
                        if cc == 0:
                            d = [wtok]
                            if bank_free[tgl] is not None:
                                d.append(bank_free[tgl])
                            if part == 2 and half == 0 and tgl == 0:
                                d += start_gate
                        mm = S.step("tensor",
                                    lambda e, part=part, tg=tg, tgl=tgl,
                                    cc=cc: e.matmul(
                            PQ[:, (2 + tgl) * 512:(2 + tgl) * 512 + 512],
                            WA[:, cc * 384 + part * 128:
                               cc * 384 + part * 128 + 128],
                            ACT1[:, cc * T + 512 * tg:
                                 cc * T + 512 * tg + 512],
                            start=(cc == 0), stop=(cc == NCH - 1)),
                            deps=d)
                    mm_done[(part, half, tgl)] = mm
                    last_mm[0] = mm
                return f

            def th_sp(part, half):
                def f():
                    dst = SB0 if part == 0 else SB1
                    tx = S.step("scalar", lambda e, half=half: e.activation(
                        CM0[:, 1024 * half:1024 * half + 1024],
                        PQ[:, 1024:2048], AF.Exp),
                        deps=[mm_done[(part, half, 0)],
                              mm_done[(part, half, 1)]] + cm_gate)
                    t2 = S.step("scalar", lambda e, dst=dst, half=half:
                                e.activation(
                        dst[:, 1024 * half:1024 * half + 1024],
                        CM0[:, 1024 * half:1024 * half + 1024],
                        AF.Ln, bias=1.0), deps=[tx])
                    mus[part].append(t2)
                    bank_free[0] = tx
                    bank_free[1] = tx
                return f

            def th_ev(half):
                def f():
                    for tgl in (0, 1):
                        pc = S.step("vector", lambda e, half=half, tgl=tgl:
                                    e.tensor_copy(
                            out=SB2[:, 1024 * half + 512 * tgl:
                                    1024 * half + 512 * tgl + 512],
                            in_=PQ[:, (2 + tgl) * 512:(2 + tgl) * 512 + 512]),
                            deps=[mm_done[(2, half, tgl)]] +
                            (sb2_gate if (half == 0 and tgl == 0) else []))
                        bank_free[tgl] = pc
                        j0 = 8 * half + 4 * tgl
                        for j in range(j0, j0 + 4):
                            for hh2 in range(2):
                                d = ()
                                if hh2 == 0 and j == j0:
                                    d = [pc, t_vt0, t_vt1] + \
                                        (vt_gate if (half == 0 and tgl == 0)
                                         else [])
                                tvt.append(S.step("sync",
                                                  lambda e, j=j, hh2=hh2,
                                                  vt=vt: e.dma_start(
                                    out=vt[:, (16 * hh2 + j) * 80:
                                           (16 * hh2 + j) * 80 + 64],
                                    in_=SB2[64 * hh2:64 * hh2 + 64,
                                            128 * j:128 * j + 128],
                                    transpose=True), dma=True, deps=d))
                return f

            for part in (2, 0, 1):
                for half in (0, 1):
                    thunks.append(th_mm(part, half, 0))
                    thunks.append(th_mm(part, half, 1))
                    if part < 2:
                        thunks.append(th_sp(part, half))
                    else:
                        thunks.append(th_ev(half))
            if p8 + 1 < NP:
                def th_load():
                    load_wa(p8 + 1, [last_mm[0]])
                thunks.append(th_load)

            def fin(prev):
                toks_h = {}
                for hh in range(2):
                    r0, r1 = 64 * hh, 64 * hh + 64
                    QS = QS0 if hh == 0 else QS1
                    KS = KS0 if hh == 0 else KS1
                    rt1 = S.step("vector", lambda e, QS=QS, r0=r0, r1=r1:
                                 e.tensor_mul(
                        out=QS[0:64, :], in0=SB0[r0:r1, :],
                        in1=TCOS[r0:r1, :]),
                        deps=mus[0] + [t_cosP] + (list(prev) if hh == 0
                                                  else []))
                    rt2 = S.step("vector", lambda e, QS=QS, r0=r0, r1=r1:
                                 e.tensor_mul(
                        out=QS[64:128, :], in0=SB0[r0:r1, :],
                        in1=TSIN[r0:r1, :]), deps=mus[0] + [t_sinP])
                    tA = S.step("vector", lambda e, r0=r0, r1=r1:
                                e.tensor_mul(
                        out=CM0[r0:r1, :], in0=SB1[r0:r1, :],
                        in1=TCOS[r0:r1, :]),
                        deps=mus[1] + [t_cosP] + (list(prev) if hh == 0
                                                  else []))
                    tB = S.step("vector", lambda e, r0=r0, r1=r1:
                                e.tensor_mul(
                        out=CM1[r0:r1, :], in0=SB1[r0:r1, :],
                        in1=TSIN[r0:r1, :]), deps=mus[1] + [t_sinP])
                    tu = S.step("vector", lambda e, p8=p8, r0=r0, r1=r1:
                                e.tensor_scalar(
                        out=SXC[r0:r1, :], in0=CM1[r0:r1, :],
                        scalar1=TRIGS[r0:r1, p8:p8 + 1],
                        scalar2=None, op0=ALU.mult),
                        deps=[tB, getattr(S, "attn_sums_gate", None)] +
                        t_trig)
                    kr = S.step("vector", lambda e, KS=KS, r0=r0, r1=r1,
                                p8=p8: e.scalar_tensor_tensor(
                        out=KS[0:64, :], in0=CM0[r0:r1, :],
                        scalar=TRIGC[r0:r1, p8:p8 + 1],
                        in1=SXC[r0:r1, :],
                        op0=ALU.mult, op1=ALU.subtract), deps=[tA, tu])
                    tv2 = S.step("vector", lambda e, p8=p8, r0=r0, r1=r1:
                                 e.tensor_scalar(
                        out=SXC[r0:r1, :], in0=CM0[r0:r1, :],
                        scalar1=TRIGS[r0:r1, p8:p8 + 1],
                        scalar2=None, op0=ALU.mult), deps=[tA, kr])
                    ki = S.step("vector", lambda e, KS=KS, r0=r0, r1=r1,
                                p8=p8: e.scalar_tensor_tensor(
                        out=KS[64:128, :], in0=CM1[r0:r1, :],
                        scalar=TRIGC[r0:r1, p8:p8 + 1],
                        in1=SXC[r0:r1, :],
                        op0=ALU.mult, op1=ALU.add), deps=[tB, tv2])
                    toks_h[hh] = S.step("vector", lambda e: e.engine_nop(),
                                        deps=[rt1, rt2, kr, ki])
                tok = (toks_h[0], toks_h[1])
                return (tok, (QS0, QS1), (KS0, KS1), vt, tvt, [toks_h[1]],
                        bank_free[0], last_mm[0])
            return thunks, fin

        wqv_tok = [None, None]

        wq_lts = {}

        def build_wqv_load(p8, deps):
            qcol = p8 * 128
            if p8 < 4:
                wstart = 256 * p8
            else:
                wstart = 256 * (p8 - 4)
            lts = {}
            for i, cc in enumerate(range(4)):
                base = i * 896
                d0 = list(deps) if i == 0 else ()
                lt = []
                lt.append(gload(WSRC[:, base:base + 128],
                                wq[cc * 128:(cc + 1) * 128,
                                   qcol:qcol + 128], deps=d0))
                lt.append(gload(WSRC[:, base + 128:base + 384],
                                wq[cc * 128:(cc + 1) * 128,
                                   wstart:wstart + 256]))
                lt.append(gload(WSRC[:, base + 384:base + 512],
                                wq[cc * 128:(cc + 1) * 128,
                                   C + qcol:C + qcol + 128]))
                lt.append(gload(WSRC[:, base + 512:base + 768],
                                wq[cc * 128:(cc + 1) * 128,
                                   C + wstart:C + wstart + 256]))
                lt.append(gload(WSRC[:, base + 768:base + 896],
                                wq[cc * 128:(cc + 1) * 128,
                                   2 * C + qcol:2 * C + qcol + 128]))
                lts[cc] = lt
            wq_lts[p8] = lts

        def build_wqv(p8, deps):
            buf = WQV0 if p8 % 2 == 0 else WQV1
            qcol = p8 * 128
            if p8 < 4:
                wstart, wodd = 256 * p8, 1
            else:
                wstart, wodd = 256 * (p8 - 4), 0
            bts = []
            for (pi, row, dstc) in ((0, CROWC, 0), (0, CROWM, 1),
                                    (1, CROWC, 2), (1, CROWM, 3)):
                srow = row[0:1, qcol:qcol + 128]
                b1 = S.step("tensor", lambda e, srow=srow: e.matmul(
                    PQ[:, 1024:1152], TONESR[:, :], srow,
                    start=True, stop=True),
                    deps=[t_crc, t_crm, t_onesr] + list(deps) +
                    ([bts[-1]] if bts else []))
                bts.append(S.step("vector", lambda e, dstc=dstc: e.tensor_copy(
                    out=CSG[:, dstc * 128:(dstc + 1) * 128],
                    in_=PQ[:, 1024:1152]),
                    deps=[b1]))
            btok = bts[-1]
            toks = []
            for half in range(2):
                ccs = list(range(half * 4, half * 4 + 4))
                if half == 0:
                    lts = wq_lts.pop(p8)
                else:
                    lts = {}
                    for i, cc in enumerate(ccs):
                        base = i * 896
                        d0 = (list(deps) + toks[-1:]) if i == 0 else ()
                        lt = []
                        lt.append(gload(WSRC[:, base:base + 128],
                                        wq[cc * 128:(cc + 1) * 128,
                                           qcol:qcol + 128], deps=d0))
                        lt.append(gload(WSRC[:, base + 128:base + 384],
                                        wq[cc * 128:(cc + 1) * 128,
                                           wstart:wstart + 256]))
                        lt.append(gload(WSRC[:, base + 384:base + 512],
                                        wq[cc * 128:(cc + 1) * 128,
                                           C + qcol:C + qcol + 128]))
                        lt.append(gload(WSRC[:, base + 512:base + 768],
                                        wq[cc * 128:(cc + 1) * 128,
                                           C + wstart:C + wstart + 256]))
                        lt.append(gload(WSRC[:, base + 768:base + 896],
                                        wq[cc * 128:(cc + 1) * 128,
                                           2 * C + qcol:2 * C + qcol + 128]))
                        lts[cc] = lt
                for i, cc in enumerate(ccs):
                    base = i * 896
                    for part in range(2):
                        mainv = WSRC[:, base + part * 384:base + part * 384 + 128]
                        ws = base + part * 384 + 128 + wodd
                        wv = WSRC[:, ws:ws + 255:2]
                        cosb = CSG[:, (2 * part) * 128:(2 * part) * 128 + 128]
                        msb = CSG[:, (2 * part + 1) * 128:(2 * part + 1) * 128 + 128]
                        dst = buf[:, cc * 384 + part * 128:cc * 384 + part * 128 + 128]
                        a1 = S.step("vector", lambda e, mainv=mainv, cosb=cosb,
                                    dst=dst, cc=cc: e.scalar_tensor_tensor(
                            out=dst, in0=mainv, scalar=TGAM[:, cc:cc + 1], in1=cosb,
                            op0=ALU.mult, op1=ALU.mult),
                            deps=lts[cc] + [btok, t_gam] +
                            ([toks[-1]] if toks else []))
                        a2 = S.step("vector", lambda e, wv=wv, msb=msb, cc=cc:
                                    e.scalar_tensor_tensor(
                            out=CSG[:, 512:640], in0=wv,
                            scalar=TGAM[:, cc:cc + 1],
                            in1=msb, op0=ALU.mult, op1=ALU.mult), deps=[a1])
                        a3 = S.step("vector", lambda e, dst=dst: e.tensor_add(
                            out=dst, in0=dst, in1=CSG[:, 512:640]), deps=[a2])
                        toks.append(a3)
                    dstv = buf[:, cc * 384 + 256:cc * 384 + 384]
                    toks.append(S.step("vector", lambda e, base=base, dstv=dstv, cc=cc:
                                       e.tensor_scalar(
                        out=dstv, in0=WSRC[:, base + 768:base + 896],
                        scalar1=TGAM[:, cc:cc + 1], scalar2=None, op0=ALU.mult),
                        deps=lts[cc]))
            wqv_tok[p8 % 2] = toks[-1]
            return btok

        def emit_bg(deps):
            t2 = S.step("scalar", lambda e: e.activation(CM0[:, 0:NCH], TGAM[:, :],
                                                         AF.Abs),
                        deps=[t_gam, t_bet] + list(deps))
            t3 = S.step("scalar", lambda e: e.activation(CM0[:, 0:NCH], CM0[:, 0:NCH],
                                                         AF.Ln), deps=[t2])
            t4 = S.step("scalar", lambda e: e.activation(CM0[:, 0:NCH], CM0[:, 0:NCH],
                                                         AF.Exp, scale=-1.0), deps=[t3])
            t5 = S.step("scalar", lambda e: e.activation(CM0[:, NCH:2 * NCH],
                                                         TGAM[:, :], AF.Sign),
                        deps=[t4])
            t6 = S.step("vector", lambda e: e.tensor_mul(
                out=CM0[:, 0:NCH], in0=CM0[:, 0:NCH], in1=CM0[:, NCH:2 * NCH]),
                deps=[t5])
            t7 = S.step("vector", lambda e: e.tensor_mul(
                out=CM0[:, 0:NCH], in0=CM0[:, 0:NCH], in1=TBET[:, :]), deps=[t6])
            return S.step("vector", lambda e: e.tensor_copy(
                out=BGCOL[:, :], in_=CM0[:, 0:NCH]), deps=[t7])

        def make_prep_b_body(bg_tok):
            def prep_b_body(p8, gates):
                vt = VT0 if p8 % 2 == 0 else VT1
                buf = WQV0 if p8 % 2 == 0 else WQV1
                thunks = []
                tvt = []
                bank_free = {0: gates.get("pq23"), 1: gates.get("pq23")}
                start_gate = list(gates.get("start") or [])
                sb2_gate = list(gates.get("sb2") or [])
                vt_gate = list(gates.get("vt") or [])
                free_gate = list(gates.get("rope") or [])
                mm_done = {}
                ev_done = {}
                last_mm = [None]
                sq_tok = {}

                if p8 == 0:
                    build_wqv_load(0, start_gate)

                def th_build():
                    btok = build_wqv(
                        p8, start_gate + free_gate +
                        ([gates["wq_free"]] if gates.get("wq_free") else []) +
                        ([gates["pq23"]] if gates.get("pq23") else []))
                    bank_free[0] = btok
                    bank_free[1] = btok
                thunks.append(th_build)

                def th_mm(part, half, tgl):
                    def f():
                        tg = 2 * half + tgl
                        mm = None
                        for cc in range(NCH):
                            d = []
                            if cc == 0:
                                d = [wqv_tok[p8 % 2]]
                                if bank_free[tgl] is not None:
                                    d.append(bank_free[tgl])
                                if part == 2 and half == 0 and tgl == 0:
                                    d += start_gate
                            mm = S.step("tensor",
                                        lambda e, part=part, tg=tg, tgl=tgl,
                                        cc=cc, buf=buf: e.matmul(
                                PQ[:, (2 + tgl) * 512:(2 + tgl) * 512 + 512],
                                buf[:, cc * 384 + part * 128:
                                    cc * 384 + part * 128 + 128],
                                ACT1[:, cc * T + 512 * tg:
                                     cc * T + 512 * tg + 512],
                                start=(cc == 0), stop=(cc == NCH - 1)),
                                deps=d)
                        mm_done[(part, half, tgl)] = mm
                        last_mm[0] = mm
                    return f

                def th_evc(part, half):
                    def f():
                        dst = (SB0, SB1, SB2)[part]
                        for tgl in (0, 1):
                            pc = S.step("vector", lambda e, dst=dst, half=half,
                                        tgl=tgl: e.tensor_copy(
                                out=dst[:, 1024 * half + 512 * tgl:
                                        1024 * half + 512 * tgl + 512],
                                in_=PQ[:, (2 + tgl) * 512:
                                       (2 + tgl) * 512 + 512]),
                                deps=[mm_done[(part, half, tgl)]] +
                                (sb2_gate if (part == 2 and half == 0 and
                                              tgl == 0) else []) +
                                (free_gate if (part != 2 and half == 0 and
                                               tgl == 0) else []))
                            bank_free[tgl] = pc
                            ev_done[(part, half, tgl)] = pc
                    return f

                def th_bias(part):
                    def f():
                        dst = (SB0, SB1, SB2)[part]
                        bm = None
                        for cc in range(NCH):
                            d = []
                            if cc == 0:
                                d = [bg_tok, wqv_tok[p8 % 2],
                                     bank_free[0], bank_free[1]]
                            bm = S.step("tensor", lambda e, cc=cc, part=part,
                                        buf=buf: e.matmul(
                                PQ[0:1, 1024:1152], BGCOL[:, cc:cc + 1],
                                buf[:, cc * 384 + part * 128:
                                    cc * 384 + part * 128 + 128],
                                start=(cc == 0), stop=(cc == NCH - 1)),
                                deps=d)
                        bev = S.step("vector", lambda e: e.tensor_copy(
                            out=BRH[0:1, 0:128], in_=PQ[0:1, 1024:1152]),
                            deps=[bm])
                        bt2 = S.step("tensor", lambda e: e.matmul(
                            PQ[:, 1536:1537], BRH[0:1, 0:128], TONESTF[:, :],
                            start=True, stop=True), deps=[bev, t_onestf])
                        bcol = S.step("vector", lambda e, part=part, p8=p8:
                                      e.tensor_copy(
                            out=BIASC[:, part * NCH + p8:part * NCH + p8 + 1],
                            in_=PQ[:, 1536:1537]), deps=[bt2])
                        bank_free[0] = bev
                        bank_free[1] = bcol
                        badds = []
                        for half in (0, 1):
                            badds.append(S.step("vector",
                                               lambda e, dst=dst, part=part,
                                               half=half, p8=p8:
                                               e.tensor_scalar(
                                out=dst[:, 1024 * half:1024 * half + 1024],
                                in0=dst[:, 1024 * half:1024 * half + 1024],
                                scalar1=BIASC[:, part * NCH + p8:
                                              part * NCH + p8 + 1],
                                scalar2=None, op0=ALU.add), deps=[bcol]))
                        if part == 2:
                            for j in range(16):
                                for hh2 in range(2):
                                    d = ()
                                    if hh2 == 0 and j % 8 == 0:
                                        d = [badds[j // 8], t_vt0, t_vt1] + \
                                            (vt_gate if j == 0 else [])
                                    tvt.append(S.step("sync",
                                                      lambda e, j=j, hh2=hh2,
                                                      vt=vt: e.dma_start(
                                        out=vt[:, (16 * hh2 + j) * 80:
                                               (16 * hh2 + j) * 80 + 64],
                                        in_=SB2[64 * hh2:64 * hh2 + 64,
                                                128 * j:128 * j + 128],
                                        transpose=True), dma=True, deps=d))
                        elif part == 0:
                            sq_tok[0] = S.step("vector", lambda e:
                                               e.tensor_mul(
                                out=CSG[:, :], in0=SB0[:, :], in1=SB0[:, :]),
                                deps=badds + free_gate)
                        else:
                            sq_tok[1] = S.step("vector",
                                               lambda e: e.engine_nop(),
                                               deps=badds + free_gate)
                    return f

                for part in (2, 0, 1):
                    for half in (0, 1):
                        thunks.append(th_mm(part, half, 0))
                        thunks.append(th_mm(part, half, 1))
                        thunks.append(th_evc(part, half))
                    thunks.append(th_bias(part))
                if p8 + 1 < NP:
                    def th_preload():
                        build_wqv_load(p8 + 1, [wqv_tok[p8 % 2]])
                    thunks.append(th_preload)

                def fin(prev):
                    nt = []
                    for g in range(4):
                        d = [sq_tok[0], sq_tok[1], t_sel2t]
                        if g == 0:
                            d += list(prev)
                        nt.append(S.step("tensor", lambda e, g=g: e.matmul(
                            PQ[0:2, 512 * g:512 * g + 512],
                            TSEL2T[:, :], CSG[:, 512 * g:512 * g + 512],
                            start=True, stop=True,
                            tile_position=(0, 0)), deps=d))
                    sqk = S.step("vector", lambda e: e.tensor_mul(
                        out=CSG[:, :], in0=SB1[:, :], in1=SB1[:, :]),
                        deps=nt + [sq_tok[1]])
                    for g in range(4):
                        nt.append(S.step("tensor", lambda e, g=g: e.matmul(
                            PQ[32:34, 512 * g:512 * g + 512],
                            TSEL2T[:, :], CSG[:, 512 * g:512 * g + 512],
                            start=True, stop=True,
                            tile_position=(0, 32)), deps=[sqk, t_sel2t]))
                    tevn = S.step("scalar", lambda e: e.copy(
                        N2[:, :], PQ[0:34, 0:2048]), deps=nt)
                    tn1a = S.step("scalar", lambda e: e.activation(
                        N2[0:2, :], N2[0:2, :], AF.Ln, bias=1e-24),
                        deps=[tevn])
                    tn1 = S.step("scalar", lambda e: e.activation(
                        N2[32:34, :], N2[32:34, :], AF.Ln, bias=1e-24),
                        deps=[tn1a])
                    ntoks = []
                    for part in range(2):
                        src = SB0 if part == 0 else SB1
                        dst = QS0 if part == 0 else KS0
                        tl2 = S.step("scalar", lambda e, part=part:
                                     e.activation(
                            CSG[0:2, :], N2[32 * part:32 * part + 2, :],
                            AF.Exp, scale=-0.5),
                            deps=[tn1] + ([ntoks[-1]] if ntoks else []))
                        for g in range(4):
                            bb = S.step("tensor", lambda e, g=g: e.matmul(
                                PQ[:, 512 * (g % 2):512 * (g % 2) + 512],
                                TSEL2[:, :], CSG[0:2, 512 * g:512 * g + 512],
                                start=True, stop=True),
                                deps=[tl2, t_sel2] +
                                ([ntoks[-1]] if ntoks else [tevn]))
                            dst2 = QS1 if part == 0 else KS1
                            ntoks.append(S.step("vector", lambda e, g=g,
                                                part=part, src=src, dst=dst:
                                                e.tensor_mul(
                                out=dst[0:64, 512 * g:512 * g + 512],
                                in0=src[0:64, 512 * g:512 * g + 512],
                                in1=PQ[0:64, 512 * (g % 2):512 * (g % 2) + 512]),
                                deps=[bb]))
                            ntoks.append(S.step("vector", lambda e, g=g,
                                                part=part, src=src, dst2=dst2:
                                                e.tensor_mul(
                                out=dst2[64:128, 512 * g:512 * g + 512],
                                in0=src[64:128, 512 * g:512 * g + 512],
                                in1=PQ[64:128,
                                       512 * (g % 2):512 * (g % 2) + 512]),
                                deps=[bb]))
                    tok = S.step("vector", lambda e: e.engine_nop(),
                                 deps=ntoks)
                    return (tok, (QS0, QS1), (KS0, KS1), vt, tvt, [tok],
                            tevn, last_mm[0])
                return thunks, fin
            return prep_b_body

        setup = [t_qkcol, t_vt0, t_vt1, t_msk, t_id] + t_trig
        sc_toks, am_last = emit_rhs_prep_scales([])
        rhs_aux = make_rhs_q_thunks(sc_toks, am_last)

        ping = [Xd0, Xd1]
        cur_src = xT

        ln_toks = emit_ln(cur_src, setup + sc_toks)
        load_wa(0, [])
        ms0 = S.step("vector", lambda e: e.memset(PS1[:, :], 0.0),
                     deps=ln_toks + sc_toks)
        ms1 = S.step("vector", lambda e: e.memset(PS2[:, :], 0.0), deps=[ms0])
        evacs = emit_attention("a", None, None, 0.125, None,
                               ln_toks + [ms1] + setup,
                               body_maker=prep_a_body, aux=rhs_aux)
        rhs_ready = emit_rhs_colsums([S.tok("scalar"), S.tok("vector")])
        ntoks = emit_norm_oall(evacs)
        stores = emit_proj(wpT, cur_src, ping[0],
                           ntoks + [e[0] for e in evacs] + rhs_ready)
        cur_src = ping[0]

        if n_sub >= 2:
            ln_toks = emit_ln(cur_src, stores, apply=False)
            stores = emit_ffn(cur_src, ping[1], ln_toks, rhs_ready)
            cur_src = ping[1]
        if n_sub >= 3:
            bg_tok = emit_bg(stores)
            ln_toks = emit_ln(cur_src, stores + [bg_tok], gamma_mode=True)
            ms0 = S.step("vector", lambda e: e.memset(PS1[:, :], 0.0), deps=ln_toks)
            ms1 = S.step("vector", lambda e: e.memset(PS2[:, :], 0.0), deps=[ms0])
            msz = [S.step("vector", lambda e, tile=tile, lohi=lohi: e.memset(
                tile[lohi[0]:lohi[1], :], 0.0), deps=list(ln_toks))
                for (tile, lohi) in ((QS0, (64, 128)), (QS1, (0, 64)),
                                     (KS0, (64, 128)), (KS1, (0, 64)))]
            evacs = emit_attention("b", None, None,
                                   None, QKCOL[:, 0:1], ln_toks + [ms1, bg_tok] + msz,
                                   body_maker=make_prep_b_body(bg_tok))
            ntoks = emit_norm_oall(evacs)
            stores = emit_proj(wo, cur_src, ping[0], ntoks + [e[0] for e in evacs])
            cur_src = ping[0]
        if n_sub >= 4:
            ln_toks = emit_ln(cur_src, stores, apply=False)
            stores = emit_ffn(cur_src, yout, ln_toks, rhs_ready)
        else:
            for cc in range(NCH):
                tl = sdma(SXC[:, :], cur_src[cc * 128:(cc + 1) * 128, :], deps=stores)
                stores = [S.step("sync", lambda e, cc=cc: e.dma_start(
                    out=yout[cc * 128:(cc + 1) * 128, :], in_=SXC[:, :]),
                    dma=True, deps=[tl])]

        S.emit(blk, sems)
    lp.__exit__(None, None, None)
    return nc


_nc_cache = {}


def kernel(L0_attn_a_x, L0_attn_a_w_attn, L0_attn_a_w_proj, L0_attn_a_delta,
           L0_attn_b_ln_gamma, L0_attn_b_ln_beta, L0_attn_b_w_qkv, L0_attn_b_w_out,
           L0_attn_b_qkscale, L0_ffn_rhs):
    x = np.asarray(L0_attn_a_x, np.float32)
    tb = host_tables()
    cc = np.ascontiguousarray

    delta = np.asarray(L0_attn_a_delta, np.float32)
    delta_pc = np.zeros((128, NP), np.float32)
    for p8 in range(NP):
        delta_pc[0:64, p8] = delta[2 * p8]
        delta_pc[64:128, p8] = delta[2 * p8 + 1]

    gamma = np.asarray(L0_attn_b_ln_gamma, np.float32)
    beta = np.asarray(L0_attn_b_ln_beta, np.float32)

    base = {
        "waT": cc(np.asarray(L0_attn_a_w_attn, np.float32).T),
        "wpT": cc(np.asarray(L0_attn_a_w_proj, np.float32).T),
        "wq": cc(np.asarray(L0_attn_b_w_qkv, np.float32)),
        "wo": cc(np.asarray(L0_attn_b_w_out, np.float32)),
        "rhs": cc(np.asarray(L0_ffn_rhs, np.float32)),
        "rhsT": cc(np.asarray(L0_ffn_rhs, np.float32).T),
        "delta_pc": delta_pc,
        "gamma_fm": cc(gamma.reshape(NCH, 128).T),
        "beta_fm": cc(beta.reshape(NCH, 128).T),
        "qks": np.asarray(L0_attn_b_qkscale, np.float32).reshape(1, 1),
    }
    base.update(tb)

    key = (N_SUBLAYERS, DBG)
    if key not in _nc_cache:
        _nc_cache[key] = build_nc(N_SUBLAYERS)
    nc = _nc_cache[key]

    in_maps = []
    for b in range(8):
        m = dict(base)
        m["xT"] = cc(x[b].T)
        in_maps.append(m)
    res = run_bass_kernel_spmd(nc, in_maps, core_ids=list(range(8)))
    out = np.stack([np.asarray(r["y"], np.float32).T for r in res.results], axis=0)
    return out
